# version 09
# baseline (speedup 1.0000x reference)
import numpy as np
from contextlib import ExitStack

import concourse.bass as bass
import concourse.mybir as mybir
from concourse.bass_utils import run_bass_kernel_spmd

F32 = mybir.dt.float32
BF16 = mybir.dt.bfloat16
I32 = mybir.dt.int32
AF = mybir.ActivationFunctionType
ALU = mybir.AluOpType
AX = mybir.AxisListType

D = 1024
T = 2048
NT = 16
NH = 16
HD = 64
DC = 8
FH = 4096
NHC = 32
GT = 256
NG = T // GT
GNT = GT // 128

TWO_PI = 2.0 * np.pi
CW1 = 6.28125
CW2 = float(np.float32(TWO_PI - CW1))
CW3 = float(TWO_PI - CW1 - np.float64(np.float32(TWO_PI - CW1)))
LN1E4 = float(np.log(10000.0))
QSCALE = float(HD ** -0.5)

ABS_SCHED = [0, 0, 6, 6, 5, 5] + [4] * 9 + [6]
assert sum(ABS_SCHED) == 64 and len(ABS_SCHED) == NT


def ap(t, off, dims):
    h = t.handle if hasattr(t, "handle") else t
    return bass.AP(h, off, dims)


class Sem:
    def __init__(self, ctx, nc, name):
        self.h = ctx.enter_context(nc.semaphore(name))
        self.n = 0


def build(debug=False, dump=False, phases=3):
    nc = bass.Bass("TRN2", target_bir_lowering=False, debug=False,
                   detect_race_conditions=False)

    x_d = nc.dram_tensor("x", [T, D], F32, kind="ExternalInput")
    wq_d = nc.dram_tensor("wqkvT", [D, 3 * D], F32, kind="ExternalInput")
    wo_d = nc.dram_tensor("woutT", [D, D], F32, kind="ExternalInput")
    w1t_d = nc.dram_tensor("w1T", [D, FH], F32, kind="ExternalInput")
    w2t_d = nc.dram_tensor("w2T", [FH, D], F32, kind="ExternalInput")
    w1n_d = nc.dram_tensor("w1n", [FH, D], F32, kind="ExternalInput")
    w2n_d = nc.dram_tensor("w2n", [D, FH], F32, kind="ExternalInput")
    b1_d = nc.dram_tensor("b1", [1, FH], F32, kind="ExternalInput")
    b2_d = nc.dram_tensor("b2", [1, D], F32, kind="ExternalInput")
    tb_d = nc.dram_tensor("tbase", [128, 1], F32, kind="ExternalInput")
    out_d = nc.dram_tensor("out", [T, D], F32, kind="ExternalOutput")
    x1_d = nc.dram_tensor("x1s", [T, D], F32,
                          kind="ExternalOutput" if dump else "Internal")
    a1_d = nc.dram_tensor("a1s", [1, FH], F32)
    a2_d = nc.dram_tensor("a2s", [1, D], F32)
    cci_d = nc.dram_tensor("cci", [65, D], F32)
    cco_d = nc.dram_tensor("cco", [65, D], F32)
    if dump:
        dmp_cc = nc.dram_tensor("dmp_cc", [65, D], F32, kind="ExternalOutput")
        dmp_qf = nc.dram_tensor("dmp_qf", [128, NT * D], F32,
                                kind="ExternalOutput")
        dmp_xln = nc.dram_tensor("dmp_xln", [128, D], F32, kind="ExternalOutput")
        dmp_q = nc.dram_tensor("dmp_q", [128, D], F32, kind="ExternalOutput")
        dmp_st = nc.dram_tensor("dmp_st", [128, 8], F32, kind="ExternalOutput")
    if debug:
        dbg_xln = nc.dram_tensor("dbg_xln", [128, D], F32, kind="ExternalOutput")
        dbg_qf = nc.dram_tensor("dbg_qf", [128, D], F32, kind="ExternalOutput")
        dbg_kf = nc.dram_tensor("dbg_kf", [128, D], F32, kind="ExternalOutput")
        dbg_cc = nc.dram_tensor("dbg_cc", [65, D], F32, kind="ExternalOutput")
        dbg_den = nc.dram_tensor("dbg_den", [128, NH], F32, kind="ExternalOutput")
        dbg_att = nc.dram_tensor("dbg_att", [128, DC * GT], F32, kind="ExternalOutput")
        dbg_x1 = nc.dram_tensor("dbg_x1", [128, D], F32, kind="ExternalOutput")

    with ExitStack() as root:
        sW = Sem(root, nc, "sW")
        sXF = [Sem(root, nc, f"sXF{j}") for j in range(4)]
        sT2 = [Sem(root, nc, f"sT2{j}") for j in range(2)]
        sTB2 = [Sem(root, nc, f"sTB2{j}") for j in range(2)]
        sBF = [Sem(root, nc, f"sBF{j}") for j in range(2)]
        sBG = [Sem(root, nc, f"sBG{j}") for j in range(2)]
        sV = Sem(root, nc, "sV")
        sA = Sem(root, nc, "sA")
        sM = Sem(root, nc, "sM")
        sG = Sem(root, nc, "sG")
        sC = Sem(root, nc, "sC")
        sB = Sem(root, nc, "sB")
        sO1 = [Sem(root, nc, f"sO1{j}") for j in range(2)]
        sO2 = [Sem(root, nc, f"sO2{j}") for j in range(2)]
        sD = Sem(root, nc, "sD")
        sVC = Sem(root, nc, "sVC")

        def vch(v, inst):
            inst.then_inc(sVC.h, 1)
            sVC.n += 1
            v.wait_ge(sVC.h, sVC.n)

        tb_sb = root.enter_context(nc.sbuf_tensor("tb", [128, 1], F32))
        b1T = root.enter_context(nc.sbuf_tensor("b1T", [128, NHC], F32))
        b2b = root.enter_context(nc.sbuf_tensor("b2b", [1, D], BF16))
        onesr = root.enter_context(nc.sbuf_tensor("onesr", [1, 128], BF16))
        onesc = root.enter_context(nc.sbuf_tensor("onesc", [128, 1], BF16))
        onesf = root.enter_context(nc.sbuf_tensor("onesf", [1, 128], F32))
        ident = root.enter_context(nc.sbuf_tensor("ident", [128, 128], BF16))
        io1 = root.enter_context(nc.sbuf_tensor("io1", [128, 128], F32))
        io2 = root.enter_context(nc.sbuf_tensor("io2", [128, 128], F32))
        zer65 = root.enter_context(nc.sbuf_tensor("zer65", [128, 65], BF16))
        a2row = root.enter_context(nc.sbuf_tensor("a2row", [1, D], F32))
        a1n = root.enter_context(nc.sbuf_tensor("a1n", [128, 32], F32))
        a2n4 = root.enter_context(nc.sbuf_tensor("a2n4", [128, 32], F32))
        a2n = root.enter_context(nc.sbuf_tensor("a2n", [128, 8], F32))
        arep2 = root.enter_context(nc.sbuf_tensor("arep2", [128, D], BF16))
        xf = [root.enter_context(nc.sbuf_tensor(f"xf{j}", [128, D], F32))
              for j in range(4)]
        st_sum = root.enter_context(nc.sbuf_tensor("st_sum", [128, 1], F32))
        st_nm = root.enter_context(nc.sbuf_tensor("st_nm", [128, 1], F32))
        st_ssq = root.enter_context(nc.sbuf_tensor("st_ssq", [128, 1], F32))
        st_v1 = root.enter_context(nc.sbuf_tensor("st_v1", [128, 1], F32))
        st_nm2 = root.enter_context(nc.sbuf_tensor("st_nm2", [128, 1], F32))
        st_var = root.enter_context(nc.sbuf_tensor("st_var", [128, 1], F32))
        st_std = root.enter_context(nc.sbuf_tensor("st_std", [128, 1], F32))
        st_rstd = root.enter_context(nc.sbuf_tensor("st_rstd", [128, 1], F32))
        den = root.enter_context(nc.sbuf_tensor("den", [128, NH], F32))
        rden = root.enter_context(nc.sbuf_tensor("rden", [128, NH], F32))
        junk = root.enter_context(nc.sbuf_tensor("junk", [128, D], F32))
        z_sc = root.enter_context(nc.sbuf_tensor("z_sc", [128, D], F32))
        if dump:
            dx_sb = root.enter_context(nc.sbuf_tensor("dx_sb", [128, D], BF16))
            dq_sb = root.enter_context(nc.sbuf_tensor("dq_sb", [128, D], BF16))
            ds_sb = root.enter_context(nc.sbuf_tensor("ds_sb", [128, 8], F32))

        B_PREP = 9
        G_PREP = 48
        G_KVB = 80
        G_KV2 = 112
        G_KSR = 128
        G_A2B = 144
        G_A2R = 160
        XFA = lambda i: 16 * (i // 4 + 1)
        XFB = lambda i: 16 * (i // 4 + 5)
        XFG = lambda i: 16 * (i // 4 + 9)
        STx = lambda i: 2 * (i // 2 + 1)
        TBQ = lambda i: 2 * (i // 2 + 1)
        TBX = lambda i: 16 + 128 * (i // 2 + 1)
        BF = lambda j: 16 * (j // 2 + 1)
        BG = lambda k: 16 * (k // 2 + 1)
        SO1 = lambda i: 16 * (i // 2 + 1)
        SO2 = lambda i: 16 * (i // 2 + 1)

        A_INVF, A_SIN, A_COS = 1, 2, 3
        A_SSQ, A_STD, A_EVQ, A_EVK = {}, {}, {}, {}
        A_EQ, A_VB, A_EK, A_ABS = {}, {}, {}, {}
        cnt = 3
        absj = 0
        for i in range(NT + 2):
            if i == 0:
                for k in range(3):
                    A_SSQ[k] = cnt + 1
                    A_STD[k] = cnt + 2
                    cnt += 2
            if i >= 2:
                jj = i - 2
                A_EQ[jj] = cnt + 1
                cnt += 1
                for _ in range(ABS_SCHED[jj]):
                    cnt += 1
                    A_ABS[absj] = cnt
                    absj += 1
                A_VB[jj] = cnt + 1
                A_EK[jj] = cnt + 2
                cnt += 2
            if 1 <= i <= NT:
                A_EVQ[i - 1] = cnt + 1
                A_EVK[i - 1] = cnt + 2
                cnt += 2
            if i >= 1 and i + 2 < NT:
                A_SSQ[i + 2] = cnt + 1
                A_STD[i + 2] = cnt + 2
                cnt += 2
        A_END_A = cnt
        assert absj == 64
        a_att = lambda g, c: A_END_A + 8 * g + c + 1
        A_END_B1 = A_END_A + 8 * NG
        SA2_SSQ, SA2_STD, SA2_GELU = {}, {}, {}
        cnt = A_END_B1
        for g in range(NG):
            for tt in range(GNT):
                i = GNT * g + tt
                SA2_SSQ[i] = cnt + 1
                SA2_STD[i] = cnt + 2
                cnt += 2
            for hc in range(NHC):
                cnt += 1
                SA2_GELU[(g, hc)] = cnt
        A_FINAL = cnt

        V_PREP = 4
        V_VARd, V_RSTDd, V_XLNd = {}, {}, {}
        V_ROPEQd, V_QFd, V_ROPEKd, V_KFd = {}, {}, {}, {}
        cnt = V_PREP
        for i in range(NT + 2):
            if i == 0:
                for k in range(2):
                    V_VARd[k] = cnt + 1
                    V_RSTDd[k] = cnt + 2
                    V_XLNd[k] = cnt + 3
                    cnt += 3
            if i >= 2:
                jj = i - 2
                V_ROPEQd[jj] = cnt + 1
                V_ROPEKd[jj] = cnt + 2
                V_QFd[jj] = cnt + 3
                V_KFd[jj] = cnt + 4
                cnt += 4
            if i + 2 < NT:
                V_VARd[i + 2] = cnt + 1
                V_RSTDd[i + 2] = cnt + 2
                V_XLNd[i + 2] = cnt + 3
                cnt += 3
        v_var = lambda i: V_VARd[i]
        v_rstd = lambda i: V_RSTDd[i]
        v_xln = lambda i: V_XLNd[i]
        v_ropeq = lambda i: V_ROPEQd[i]
        v_qf = lambda i: V_QFd[i]
        v_ropek = lambda i: V_ROPEKd[i]
        v_kf = lambda i: V_KFd[i]
        V_KVEV = cnt + 1
        V_KSB = V_KVEV + 1
        V_DEN = V_KSB + 1
        VL_QFD, VL_X1, VL_BIN1 = {}, {}, {}
        cur = V_DEN
        for g in range(NG):
            for tt in range(GNT):
                cur += 1
                VL_QFD[GNT * g + tt] = cur
            for kk in range(2):
                cur += 1
                VL_BIN1[4 * g + kk] = cur
            for tt in range(GNT):
                cur += 1
                VL_X1[GNT * g + tt] = cur
            for kk in range(2, 4):
                cur += 1
                VL_BIN1[4 * g + kk] = cur
        v_qfd = lambda i: VL_QFD[i]
        v_x1 = lambda i: VL_X1[i]
        V_A1F = cur + 1
        V_AREP2 = V_A1F + 1
        VL_BIN2, VL_M2 = {}, {}
        VL_V2VAR, VL_V2RSTD, VL_V2XLN, VL_V2X2 = {}, {}, {}, {}
        cur = V_AREP2

        def _ln2p(i):
            global_ = None
            return i

        for gg in range(2):
            for tt in range(GNT):
                i = GNT * gg + tt
                cur += 3
                VL_V2VAR[i] = cur - 2
                VL_V2RSTD[i] = cur - 1
                VL_V2XLN[i] = cur
        for k in range(NHC):
            cur += 1
            VL_BIN2[k] = cur
        for hc in range(NHC):
            cur += 1
            VL_M2[hc] = cur
        for g in range(NG):
            for tt in range(GNT):
                i = GNT * g + tt
                cur += 1
                VL_V2X2[i] = cur
            if g + 2 < NG:
                for tt in range(GNT):
                    i = GNT * (g + 2) + tt
                    cur += 3
                    VL_V2VAR[i] = cur - 2
                    VL_V2RSTD[i] = cur - 1
                    VL_V2XLN[i] = cur
        V_FINAL = cur

        M_Qd, M_Kd, M_Vd, M_KVd, M_KSd, M_TPA = {}, {}, {}, {}, {}, {}
        cnt = 0
        for j in range(NT):
            for c in range(DC):
                cnt += 1
                M_TPA[(j, c)] = cnt
            M_Qd[j] = cnt + 1
            M_Kd[j] = cnt + 2
            cnt += 2
            if j >= 1:
                M_KVd[j - 1] = cnt + 1
                M_KSd[j - 1] = cnt + 2
                cnt += 2
            M_Vd[j] = cnt + 1
            cnt += 1
        M_KVd[NT - 1] = cnt + 1
        M_KSd[NT - 1] = cnt + 2
        cnt += 2
        M_END_A = cnt + 1
        m_qp = lambda i: M_Qd[i]
        m_kp = lambda i: M_Kd[i]
        m_vp = lambda i: M_Vd[i]
        m_tp1 = lambda i, c: (M_END_A + (16 + 8 + GNT) * (i // GNT)
                              + 8 * (i % GNT) + c + 1)
        m_att = lambda g, c: M_END_A + (16 + 8 + GNT) * g + 16 + c + 1
        m_op = lambda g, tt: M_END_A + (16 + 8 + GNT) * g + 24 + tt + 1
        M_END_B1 = M_END_A + (16 + 8 + GNT) * NG
        M_B2BC = M_END_B1 + 1
        m_ffn1 = lambda g, hc: M_B2BC + (NHC + GNT) * g + hc + 1
        m_ffn2 = lambda g, tt: M_B2BC + (NHC + GNT) * g + NHC + tt + 1

        LAST_A_XF = {0: 12, 1: 13, 2: 14, 3: 15}

        ab1 = root.enter_context(ExitStack())
        qf_all = ab1.enter_context(nc.sbuf_tensor("qf_all", [128, NT * D], BF16,
                                                  side="right"))
        wo_sb = [ab1.enter_context(nc.sbuf_tensor(f"wo{c}", [128, D], BF16,
                                                  side="right"))
                 for c in range(DC)]
        kv2 = ab1.enter_context(nc.sbuf_tensor("kv2", [128, 8 * HD], BF16,
                                               side="right"))
        ksb = ab1.enter_context(nc.sbuf_tensor("ksb", [128, D], F32,
                                               side="right"))
        binf = [ab1.enter_context(nc.sbuf_tensor(f"binf{j}", [128, D], F32,
                                                 side="right"))
                for j in range(2)]
        ksrow = ab1.enter_context(nc.sbuf_tensor("ksrow", [1, D], F32,
                                                 side="right"))

        with ExitStack() as actx:
            cos_t = actx.enter_context(nc.sbuf_tensor("cos_t", [128, 512], F32))
            kvev = actx.enter_context(nc.sbuf_tensor("kvev", [64, D], F32))
            ksev = actx.enter_context(nc.sbuf_tensor("ksev", [1, 1024], F32))
            ssin = actx.enter_context(nc.sbuf_tensor("ssin", [128, 1024], F32))
            tgT = actx.enter_context(nc.sbuf_tensor("tgT", [128, 16], F32))
            tgI = actx.enter_context(nc.sbuf_tensor("tgI", [128, 32], F32))
            tgF = actx.enter_context(nc.sbuf_tensor("tgF", [128, 32], F32))
            ni = actx.enter_context(nc.sbuf_tensor("ni", [128, 512], I32))
            wq_sb = [actx.enter_context(nc.sbuf_tensor(f"wq{c}", [128, 3 * D], BF16))
                     for c in range(DC)]
            xln_b = [actx.enter_context(nc.sbuf_tensor(f"xln{j}", [128, D], BF16))
                     for j in range(2)]
            xlnT = [actx.enter_context(nc.sbuf_tensor(f"xlnT{j}", [128, D], BF16))
                    for j in range(2)]
            q_sb = actx.enter_context(nc.sbuf_tensor("q_sb", [128, D], F32))
            k_sb = actx.enter_context(nc.sbuf_tensor("k_sb", [128, D], F32))
            q_raw = actx.enter_context(nc.sbuf_tensor("q_raw", [128, D], F32))
            k_raw = actx.enter_context(nc.sbuf_tensor("k_raw", [128, D], F32))
            rp_sc = actx.enter_context(nc.sbuf_tensor("rp_sc", [128, D], F32))
            e_sc = actx.enter_context(nc.sbuf_tensor("e_sc", [128, D], BF16))
            r_sc = actx.enter_context(nc.sbuf_tensor("r_sc", [128, D], BF16))
            e2_sc = actx.enter_context(nc.sbuf_tensor("e2_sc", [128, D], BF16))
            r2_sc = actx.enter_context(nc.sbuf_tensor("r2_sc", [128, D], BF16))
            kf_b = actx.enter_context(nc.sbuf_tensor("kf_b", [128, D], BF16))
            v_b = actx.enter_context(nc.sbuf_tensor("v_b", [128, D], BF16))

            ps_a = actx.enter_context(nc.psum_tensor("ps_a", [128, D], F32))
            ps_b = actx.enter_context(nc.psum_tensor("ps_b", [128, D], F32))
            kv_ps = [actx.enter_context(nc.psum_tensor(f"kvps{j}", [65, 512], F32))
                     for j in range(2)]
            tp_ps = [actx.enter_context(nc.psum_tensor(f"tpps{j}", [128, 512],
                                                       BF16)) for j in range(2)]

            blockA = actx.enter_context(nc.Block("A"))

            def g_prep(g):
                g.iota(tgT[:, :], [[128, 16]], base=0, channel_multiplier=1,
                       allow_small_or_imprecise_dtypes=True).then_inc(sB.h, 1)
                g.iota(tgI[:, :], [[1, 32]], base=0, channel_multiplier=0,
                       allow_small_or_imprecise_dtypes=True).then_inc(sB.h, 1)
                g.memset(onesc[:, :], 1.0).then_inc(sB.h, 1)
                g.memset(onesr[:, :], 1.0).then_inc(sB.h, 1)
                g.memset(onesf[:, :], 1.0).then_inc(sB.h, 1)
                g.memset(zer65[:, :], 0.0).then_inc(sB.h, 1)
                g.memset(xlnT[0][:, :], 0.0).then_inc(sB.h, 1)
                g.iota(io1[:, :], [[1, 128]], base=0, channel_multiplier=0,
                       allow_small_or_imprecise_dtypes=True).then_inc(sB.h, 1)
                g.iota(io2[:, :], [[0, 128]], base=0, channel_multiplier=1,
                       allow_small_or_imprecise_dtypes=True).then_inc(sB.h, 1)
                sB.n += 8
                g.dma_start(out=tb_sb[:, :], in_=tb_d.ap()).then_inc(sG.h, 16)
                g.dma_start(out=b1T[:, :],
                            in_=ap(b1_d, 0, [[1, 128], [128, 32]]),
                            allow_slow_non_contiguous=True).then_inc(sG.h, 16)
                g.dma_start(out=b2b[:, :], in_=b2_d.ap()).then_inc(sG.h, 16)
                sG.n += 48
                assert sG.n == G_PREP
                for c in range(DC):
                    g.dma_start(out=wq_sb[c][:, :],
                                in_=wq_d.ap()[128 * c:128 * (c + 1), :]
                                ).then_inc(sW.h, 16)
                    sW.n += 16
                for c in range(DC):
                    g.dma_start(out=wo_sb[c][:, :],
                                in_=wo_d.ap()[128 * c:128 * (c + 1), :]
                                ).then_inc(sW.h, 16)
                    sW.n += 16
                for k in range(32):
                    if k >= 2:
                        g.wait_ge(sA.h, A_ABS[k - 2])
                    g.dma_start(out=binf[k % 2][:, :],
                                in_=w1n_d.ap()[128 * k:128 * (k + 1), :]
                                ).then_inc(sBF[k % 2].h, 16)
                    sBF[k % 2].n += 16
                for k in range(32):
                    r_, s_ = divmod(k, 4)
                    j = 32 + k
                    g.wait_ge(sA.h, A_ABS[j - 2])
                    g.dma_start(out=binf[j % 2][:, :],
                                in_=w2n_d.ap()[128 * r_:128 * (r_ + 1),
                                               1024 * s_:1024 * (s_ + 1)]
                                ).then_inc(sBF[j % 2].h, 16)
                    sBF[j % 2].n += 16

            blockA.gpsimd(g_prep)

            def v_prep(v):
                v.wait_ge(sB.h, B_PREP)
                v.tensor_tensor(ident[:, :], io1[:, :], io2[:, :],
                                ALU.is_equal)
                v.wait_ge(sG.h, G_PREP)
                v.tensor_scalar_add(tgT[:, :], tgT[:, :], tb_sb[:, :]
                                    ).then_inc(sV.h, 1)
                sV.n += 1
                v.wait_ge(sV.h, 1)
                v.wait_ge(sA.h, A_INVF)
                last = None
                for j in range(16):
                    last = v.tensor_scalar_mul(q_raw[:, 32 * j:32 * (j + 1)],
                                               tgF[:, :], tgT[:, j:j + 1])
                vch(v, last)
                vch(v, v.tensor_scalar_mul(k_raw[:, 0:512], q_raw[:, 0:512], 1.0 / TWO_PI))
                vch(v, v.tensor_copy(ni[:, :], k_raw[:, 0:512]))
                vch(v, v.tensor_copy(k_raw[:, 0:512], ni[:, :]))
                vch(v, v.scalar_tensor_tensor(q_raw[:, 512:1024], k_raw[:, 0:512], -CW1,
                                              q_raw[:, 0:512], ALU.mult, ALU.add))
                vch(v, v.scalar_tensor_tensor(q_raw[:, 512:1024], k_raw[:, 0:512], -CW2,
                                              q_raw[:, 512:1024], ALU.mult, ALU.add))
                vch(v, v.scalar_tensor_tensor(q_raw[:, 512:1024], k_raw[:, 0:512], -CW3,
                                              q_raw[:, 512:1024], ALU.mult, ALU.add))
                vch(v, v.tensor_single_scalar(z_sc[:, 0:512], q_raw[:, 512:1024],
                                              float(np.pi), ALU.is_gt))
                vch(v, v.scalar_tensor_tensor(q_raw[:, 512:1024], z_sc[:, 0:512], -CW1,
                                              q_raw[:, 512:1024], ALU.mult, ALU.add))
                vch(v, v.scalar_tensor_tensor(q_raw[:, 512:1024], z_sc[:, 0:512], -CW2,
                                              q_raw[:, 512:1024], ALU.mult, ALU.add))
                v.scalar_tensor_tensor(q_raw[:, 512:1024], z_sc[:, 0:512], -CW3,
                                       q_raw[:, 512:1024], ALU.mult, ALU.add
                                       ).then_inc(sV.h, 1)
                sV.n += 1
                v.wait_ge(sV.h, sV.n)
                v.wait_ge(sA.h, A_SIN)
                vch(v, v.tensor_scalar_add(k_raw[:, 0:512], q_raw[:, 512:1024],
                                           float(np.pi / 2)))
                vch(v, v.tensor_single_scalar(z_sc[:, 0:512], k_raw[:, 0:512],
                                              float(np.pi), ALU.is_gt))
                vch(v, v.scalar_tensor_tensor(k_raw[:, 0:512], z_sc[:, 0:512], -CW1,
                                              k_raw[:, 0:512], ALU.mult, ALU.add))
                vch(v, v.scalar_tensor_tensor(k_raw[:, 0:512], z_sc[:, 0:512], -CW2,
                                              k_raw[:, 0:512], ALU.mult, ALU.add))
                v.scalar_tensor_tensor(k_raw[:, 0:512], z_sc[:, 0:512], -CW3,
                                       k_raw[:, 0:512], ALU.mult, ALU.add
                                       ).then_inc(sV.h, 1)
                sV.n += 1
                v.wait_ge(sV.h, sV.n)
                v.tensor_scalar_mul(
                    ap(ssin, 0, [[1024, 128], [64, 16], [1, 32]]),
                    ap(q_raw, 0, [[1024, 128], [32, 16], [1, 32]]), -1.0)
                v.tensor_copy(
                    ap(ssin, 32, [[1024, 128], [64, 16], [1, 32]]),
                    ap(q_raw, 0, [[1024, 128], [32, 16], [1, 32]])).then_inc(sV.h, 1)
                sV.n += 1
                assert sV.n == V_PREP

            blockA.vector(v_prep)

            def s_prep(s):
                s.wait_ge(sB.h, 2)
                s.activation(tgF[:, :], tgI[:, :], AF.Exp,
                             scale=-LN1E4 / 32.0).then_inc(sA.h, 1)
                sA.n += 1
                s.wait_ge(sV.h, 2)
                s.activation(q_raw[:, 0:512], q_raw[:, 512:1024], AF.Sin).then_inc(sA.h, 1)
                sA.n += 1
                s.wait_ge(sV.h, 3)
                s.activation(cos_t[:, :], k_raw[:, 0:512], AF.Sin).then_inc(sA.h, 1)
                sA.n += 1
                assert sA.n == A_COS

            blockA.scalar(s_prep)

            def sy_A(sy):
                for i in range(4):
                    sy.dma_start(out=xf[i][:, :],
                                 in_=x_d.ap()[128 * i:128 * (i + 1), :]
                                 ).then_inc(sXF[i].h, 16)
                    sXF[i].n += 16
                for i in range(NT):
                    if i + 4 < NT:
                        sy.wait_ge(sV.h, v_xln(i))
                        sy.wait_ge(sA.h, A_SSQ[i])
                        sy.dma_start(out=xf[(i + 4) % 4][:, :],
                                     in_=x_d.ap()[128 * (i + 4):128 * (i + 5), :]
                                     ).then_inc(sXF[(i + 4) % 4].h, 16)
                        sXF[(i + 4) % 4].n += 16

            blockA.sync(sy_A)

            def v_A(v):
                d4 = [[1024, 128], [64, 16], [32, 2], [1, 32]]

                def ln_stage(i):
                    xt = xf[i % 4]
                    v.wait_ge(sXF[i % 4].h, XFA(i))
                    vch(v, v.reduce_sum(st_sum[:, :], xt[:, :], axis=AX.X))
                    vch(v, v.tensor_scalar_mul(st_nm[:, :], st_sum[:, :],
                                               -1.0 / D))
                    v.wait_ge(sA.h, A_SSQ[i])
                    v.tensor_scalar_mul(st_v1[:, :], st_ssq[:, :], 1.0 / D)
                    vch(v, v.tensor_mul(st_nm2[:, :], st_nm[:, :], st_nm[:, :]))
                    v.scalar_tensor_tensor(st_var[:, :], st_v1[:, :], 1e-5,
                                           st_nm2[:, :], ALU.add, ALU.subtract
                                           ).then_inc(sV.h, 1)
                    sV.n += 1
                    assert sV.n == v_var(i)
                    v.wait_ge(sA.h, A_STD[i])
                    v.reciprocal(st_rstd[:, :], st_std[:, :]).then_inc(sV.h, 1)
                    sV.n += 1
                    v.wait_ge(sV.h, v_rstd(i))
                    if i >= 2:
                        v.wait_ge(sM.h, M_TPA[(i - 2, DC - 1)])
                    vch(v, v.tensor_scalar_add(z_sc[:, :], xt[:, :],
                                               st_nm[:, :]))
                    v.tensor_scalar_mul(xln_b[i % 2][:, :], z_sc[:, :],
                                        st_rstd[:, :]).then_inc(sV.h, 1)
                    sV.n += 1
                    assert sV.n == v_xln(i)

                def attn_stage(j):
                    cos_ap = ap(cos_t, 32 * j,
                                [[512, 128], [0, 16], [0, 2], [1, 32]])
                    sin_ap = ap(ssin, 64 * j,
                                [[1024, 128], [0, 16], [32, 2], [1, 32]])
                    if j == 0:
                        v.wait_ge(sA.h, A_COS)
                    v.wait_ge(sA.h, A_EVQ[j])
                    if j >= 1:
                        v.wait_ge(sA.h, A_EQ[j - 1])
                    v.tensor_tensor(ap(q_sb, 0, d4), ap(q_raw, 0, d4),
                                    cos_ap, ALU.mult)
                    swap = ap(q_raw, 32,
                              [[1024, 128], [64, 16], [-32, 2], [1, 32]])
                    vch(v, v.tensor_tensor(ap(rp_sc, 0, d4), swap, sin_ap,
                                           ALU.mult))
                    v.tensor_add(q_sb[:, :], q_sb[:, :], rp_sc[:, :]
                                 ).then_inc(sV.h, 1)
                    sV.n += 1
                    assert sV.n == v_ropeq(j)
                    v.wait_ge(sA.h, A_EVK[j])
                    if j >= 1:
                        v.wait_ge(sA.h, A_EK[j - 1])
                    v.tensor_tensor(ap(k_sb, 0, d4), ap(k_raw, 0, d4),
                                    cos_ap, ALU.mult)
                    swapk = ap(k_raw, 32,
                               [[1024, 128], [64, 16], [-32, 2], [1, 32]])
                    vch(v, v.tensor_tensor(ap(rp_sc, 0, d4), swapk, sin_ap,
                                           ALU.mult))
                    v.tensor_add(k_sb[:, :], k_sb[:, :], rp_sc[:, :]
                                 ).then_inc(sV.h, 1)
                    sV.n += 1
                    assert sV.n == v_ropek(j)
                    v.wait_ge(sA.h, A_EQ[j])
                    v.scalar_tensor_tensor(qf_all[:, D * j:D * (j + 1)],
                                           e_sc[:, :], 1.0, r_sc[:, :],
                                           ALU.min, ALU.add).then_inc(sV.h, 1)
                    sV.n += 1
                    assert sV.n == v_qf(j)
                    v.wait_ge(sA.h, A_EK[j])
                    if j >= 1:
                        v.wait_ge(sM.h, M_KVd[j - 1])
                    v.scalar_tensor_tensor(kf_b[:, :], e2_sc[:, :], 1.0,
                                           r2_sc[:, :], ALU.min, ALU.add
                                           ).then_inc(sV.h, 1)
                    sV.n += 1
                    assert sV.n == v_kf(j)

                for i in range(NT + 2):
                    if i == 0:
                        ln_stage(0)
                        ln_stage(1)
                    if i >= 2:
                        attn_stage(i - 2)
                    if i < NT:
                        v.wait_ge(sM.h, M_TPA[(i, DC - 1)])
                        for bk in range(2):
                            v.tensor_copy(
                                ap(xlnT[i % 2], 128 * bk,
                                   [[1024, 128], [256, 4], [1, 128]]),
                                ap(tp_ps[bk], 0,
                                   [[512, 128], [128, 4], [1, 128]])
                                ).then_inc(sT2[i % 2].h, 1)
                            sT2[i % 2].n += 1
                    if i + 2 < NT:
                        ln_stage(i + 2)
                v.wait_ge(sM.h, M_END_A - 1)
                v.tensor_copy(kvev[:, 0:512], kv_ps[0][0:64, :])
                v.tensor_copy(kvev[:, 512:1024], kv_ps[1][0:64, :])
                v.tensor_copy(ksev[:, 0:512], kv_ps[0][64:65, :])
                v.tensor_copy(ksev[:, 512:1024], kv_ps[1][64:65, :]
                              ).then_inc(sV.h, 1)
                sV.n += 1
                assert sV.n == V_KVEV
                v.wait_ge(sM.h, M_END_A)
                v.tensor_copy(ksb[:, :], ps_a[:, :]).then_inc(sV.h, 1)
                sV.n += 1
                assert sV.n == V_KSB

            blockA.vector(v_A)

            def s_A(s):
                def ssq_std(k):
                    xt = xf[k % 4]
                    s.wait_ge(sXF[k % 4].h, XFA(k))
                    if k >= 1:
                        s.wait_ge(sV.h, v_var(k - 1))
                    s.activation(junk[:, :], xt[:, :], AF.Square,
                                 accum_out=st_ssq[:, :]).then_inc(sA.h, 1)
                    sA.n += 1
                    assert sA.n == A_SSQ[k]
                    s.wait_ge(sV.h, v_var(k))
                    s.activation(st_std[:, :], st_var[:, :], AF.Sqrt
                                 ).then_inc(sA.h, 1)
                    sA.n += 1
                    assert sA.n == A_STD[k]

                absj_ = 0
                for i in range(NT + 2):
                    if i == 0:
                        for k in range(3):
                            ssq_std(k)
                    if i >= 2:
                        j = i - 2
                        psq = ps_a if j % 2 == 0 else ps_b
                        s.wait_ge(sV.h, v_ropeq(j))
                        if j >= 1:
                            s.wait_ge(sV.h, v_qf(j - 1))
                        s.activation(e_sc[:, :], q_sb[:, :], AF.Exp,
                                     scale=QSCALE)
                        s.activation(r_sc[:, :], q_sb[:, :], AF.Relu,
                                     scale=QSCALE).then_inc(sA.h, 1)
                        sA.n += 1
                        assert sA.n == A_EQ[j]
                        for _ in range(ABS_SCHED[j]):
                            jj = absj_
                            s.wait_ge(sBF[jj % 2].h, BF(jj))
                            dst = (a1n[:, jj:jj + 1] if jj < 32
                                   else a2n4[:, jj - 32:jj - 31])
                            s.activation(junk[:, :], binf[jj % 2][:, :],
                                         AF.Abs, accum_out=dst
                                         ).then_inc(sA.h, 1)
                            sA.n += 1
                            absj_ += 1
                        s.wait_ge(sM.h, M_Vd[j])
                        s.wait_ge(sV.h, v_qf(j))
                        s.activation(v_b[:, :], psq[:, :], AF.Copy
                                     ).then_inc(sA.h, 1)
                        sA.n += 1
                        assert sA.n == A_VB[j]
                        s.wait_ge(sV.h, v_ropek(j))
                        if j >= 1:
                            s.wait_ge(sV.h, v_kf(j - 1))
                        s.activation(e2_sc[:, :], k_sb[:, :], AF.Exp)
                        s.activation(r2_sc[:, :], k_sb[:, :], AF.Relu
                                     ).then_inc(sA.h, 1)
                        sA.n += 1
                        assert sA.n == A_EK[j]
                    if 1 <= i <= NT:
                        j = i - 1
                        psq = ps_a if j % 2 == 0 else ps_b
                        psk = ps_b if j % 2 == 0 else ps_a
                        s.wait_ge(sM.h, M_Qd[j])
                        if j >= 1:
                            s.wait_ge(sV.h, v_ropeq(j - 1))
                        s.activation(q_raw[:, :], psq[:, :], AF.Copy
                                     ).then_inc(sA.h, 1)
                        sA.n += 1
                        assert sA.n == A_EVQ[j]
                        s.wait_ge(sM.h, M_Kd[j])
                        if j >= 1:
                            s.wait_ge(sV.h, v_ropek(j - 1))
                        s.activation(k_raw[:, :], psk[:, :], AF.Copy
                                     ).then_inc(sA.h, 1)
                        sA.n += 1
                        assert sA.n == A_EVK[j]
                    if i >= 1 and i + 2 < NT:
                        ssq_std(i + 2)
                assert sA.n == A_END_A

            blockA.scalar(s_A)

            def t_A(t):
                t.wait_ge(sB.h, B_PREP)
                for n in range(2):
                    t.matmul(kv_ps[n][0:65, :], zer65[:, :],
                             xlnT[0][:, 0:512], start=True, stop=True,
                             skip_group_check=True)

                def kv_mms(j):
                    t.wait_ge(sV.h, v_kf(j))
                    t.wait_ge(sA.h, A_VB[j])
                    last = None
                    for h in range(NH):
                        last = t.matmul(
                            kv_ps[h // 8][0:64, 64 * (h % 8):64 * (h % 8) + 64],
                            kf_b[:, 64 * h:64 * (h + 1)],
                            v_b[:, 64 * h:64 * (h + 1)],
                            start=False, stop=(j == NT - 1),
                            skip_group_check=True)
                    last.then_inc(sM.h, 1)
                    sM.n += 1
                    assert sM.n == M_KVd[j]
                    for n in range(2):
                        last = t.matmul(kv_ps[n][64:65, 0:512],
                                        onesc[:, :],
                                        kf_b[:, 512 * n:512 * (n + 1)],
                                        start=False, stop=(j == NT - 1),
                                        tile_position=(0, 64),
                                        skip_group_check=True)
                    last.then_inc(sM.h, 1)
                    sM.n += 1
                    assert sM.n == M_KSd[j]

                for i in range(NT):
                    psq = ps_a if i % 2 == 0 else ps_b
                    psk = ps_b if i % 2 == 0 else ps_a
                    xT = xlnT[i % 2]
                    t.wait_ge(sV.h, v_xln(i))
                    if i >= 1:
                        t.wait_ge(sT2[(i - 1) % 2].h, STx(i - 1))
                    for c in range(DC):
                        t.transpose(tp_ps[c % 2][:, 128 * (c // 2):
                                                 128 * (c // 2) + 128],
                                    xln_b[i % 2][:, 128 * c:128 * (c + 1)],
                                    ident[:, :]).then_inc(sM.h, 1)
                        sM.n += 1
                        assert sM.n == M_TPA[(i, c)]
                    t.wait_ge(sT2[i % 2].h, STx(i))
                    if i == 0:
                        t.wait_ge(sW.h, 256)
                    if i >= 1:
                        t.wait_ge(sA.h, A_EVK[i - 1])
                    last = None
                    for c in range(DC):
                        for n in range(2):
                            last = t.matmul(psq[:, 512 * n:512 * (n + 1)],
                                            xT[:, 128 * c:128 * (c + 1)],
                                            wq_sb[c][:, 512 * n:512 * (n + 1)],
                                            start=(c == 0), stop=(c == DC - 1),
                                            skip_group_check=True)
                    last.then_inc(sM.h, 1)
                    sM.n += 1
                    assert sM.n == M_Qd[i]
                    if i >= 1:
                        t.wait_ge(sA.h, A_VB[i - 1])
                    for c in range(DC):
                        for n in range(2):
                            last = t.matmul(
                                psk[:, 512 * n:512 * (n + 1)],
                                xT[:, 128 * c:128 * (c + 1)],
                                wq_sb[c][:, 1024 + 512 * n:1024 + 512 * (n + 1)],
                                start=(c == 0), stop=(c == DC - 1),
                                skip_group_check=True)
                    last.then_inc(sM.h, 1)
                    sM.n += 1
                    assert sM.n == M_Kd[i]
                    if i >= 1:
                        kv_mms(i - 1)
                    t.wait_ge(sA.h, A_EVQ[i])
                    for c in range(DC):
                        for n in range(2):
                            last = t.matmul(
                                psq[:, 512 * n:512 * (n + 1)],
                                xT[:, 128 * c:128 * (c + 1)],
                                wq_sb[c][:, 2048 + 512 * n:2048 + 512 * (n + 1)],
                                start=(c == 0), stop=(c == DC - 1),
                                skip_group_check=True)
                    last.then_inc(sM.h, 1)
                    sM.n += 1
                    assert sM.n == M_Vd[i]
                kv_mms(NT - 1)
                t.wait_ge(sG.h, G_KSR)
                t.wait_ge(sA.h, A_EVK[NT - 1])
                last = None
                for n in range(2):
                    last = t.matmul(ps_a[:, 512 * n:512 * (n + 1)],
                                    onesf[0:1, :],
                                    ksrow[0:1, 512 * n:512 * (n + 1)],
                                    start=True, stop=True,
                                    skip_group_check=True)
                last.then_inc(sM.h, 1)
                sM.n += 1
                assert sM.n == M_END_A

            blockA.tensor(t_A)

            def g_cc(g):
                g.wait_ge(sV.h, V_KVEV)
                g.dma_start(out=ap(cci_d, 0, [[1024, 64], [1, 1024]]),
                            in_=kvev[:, :]).then_inc(sG.h, 16)
                g.dma_start(out=cci_d.ap()[64:65, :],
                            in_=ksev[:, :]).then_inc(sG.h, 16)
                sG.n += 32
                assert sG.n == G_KVB
                g.wait_ge(sG.h, G_KVB)
                g.collective_compute(
                    "AllReduce", ALU.add,
                    replica_groups=[[0, 1, 2, 3], [4, 5, 6, 7]],
                    ins=[cci_d.ap().opt()],
                    outs=[cco_d.ap().opt()],
                ).then_inc(sC.h, 1)
                sC.n += 1
                g.wait_ge(sC.h, 1)
                g.dma_start(out=kv2[0:64, :].rearrange("p (c v) -> p c v", v=64),
                            in_=ap(cco_d, 0, [[1024, 64], [128, 8], [1, 64]])
                            ).then_inc(sG.h, 16)
                g.dma_start(out=kv2[64:128, :].rearrange("p (c v) -> p c v", v=64),
                            in_=ap(cco_d, 64, [[1024, 64], [128, 8], [1, 64]])
                            ).then_inc(sG.h, 16)
                sG.n += 32
                assert sG.n == G_KV2
                g.dma_start(out=ksrow[:, :],
                            in_=cco_d.ap()[64:65, :]).then_inc(sG.h, 16)
                sG.n += 16
                assert sG.n == G_KSR
                if dump:
                    g.dma_start(out=dmp_cc.ap(), in_=cco_d.ap()).then_inc(sD.h, 16)
                    g.dma_start(out=dmp_qf.ap(), in_=qf_all[:, :]
                                ).then_inc(sD.h, 16)
                    g.dma_start(out=dmp_xln.ap(), in_=dx_sb[:, :]
                                ).then_inc(sD.h, 16)
                    g.dma_start(out=dmp_q.ap(), in_=dq_sb[:, :]
                                ).then_inc(sD.h, 16)
                    g.dma_start(out=dmp_st.ap(), in_=ds_sb[:, :]
                                ).then_inc(sD.h, 16)
                    g.wait_ge(sD.h, 80)

            blockA.gpsimd(g_cc)

        if phases < 2:
            return nc
        bw1 = [root.enter_context(nc.sbuf_tensor(f"bw1_{c}", [128, FH], BF16))
               for c in range(DC)]
        binf2 = [root.enter_context(nc.sbuf_tensor(f"binf2_{j}", [128, D], F32))
                 for j in range(2)]
        blockB = root.enter_context(nc.Block("B"))

        with ExitStack() as b1ctx:
            arep1 = b1ctx.enter_context(nc.sbuf_tensor("arep1", [128, FH],
                                                        BF16))
            qfd = [b1ctx.enter_context(nc.sbuf_tensor(f"qfd{j}", [128, D], BF16))
                   for j in range(2)]
            z_sc2 = b1ctx.enter_context(nc.sbuf_tensor("z_sc2", [128, D], F32))
            den_all = b1ctx.enter_context(nc.sbuf_tensor("den_all",
                                                         [128, NT * NH], F32))
            rden_all = b1ctx.enter_context(nc.sbuf_tensor("rden_all",
                                                          [128, NT * NH], F32))
            qfdT = b1ctx.enter_context(nc.sbuf_tensor("qfdT", [128, DC * GT], BF16))
            attT = b1ctx.enter_context(nc.sbuf_tensor("attT", [128, DC * GT], BF16))
            x1sb = [b1ctx.enter_context(nc.sbuf_tensor(f"x1sb{j}", [128, D], F32))
                    for j in range(2)]
            att_ps = [b1ctx.enter_context(nc.psum_tensor(f"attps{j}", [128, 512],
                                                         F32)) for j in range(4)]
            op_ps = b1ctx.enter_context(nc.psum_tensor("opps", [128, D], F32))
            tp2 = [b1ctx.enter_context(nc.psum_tensor(f"tp2_{j}", [128, 512],
                                                      BF16)) for j in range(2)]

            def g_B1(g):
                for k in range(32):
                    qq, dc = divmod(k, 8)
                    if k >= 2:
                        g.wait_ge(sV.h, VL_BIN1[k - 2])
                    g.dma_start(out=binf2[k % 2][:, :],
                                in_=w1t_d.ap()[128 * dc:128 * (dc + 1),
                                               1024 * qq:1024 * (qq + 1)]
                                ).then_inc(sBG[k % 2].h, 16)
                    sBG[k % 2].n += 16
                g.wait_ge(sV.h, V_A1F)
                g.dma_start(out=ap(a2_d, 0, [[1, 128], [128, 8]]),
                            in_=a2n[:, :],
                            allow_slow_non_contiguous=True).then_inc(sG.h, 16)
                sG.n += 16
                assert sG.n == G_A2B
                g.wait_ge(sG.h, G_A2B)
                g.dma_start(out=a2row[:, :], in_=a2_d.ap()).then_inc(sG.h, 16)
                sG.n += 16
                assert sG.n == G_A2R

            blockB.gpsimd(g_B1)

            def sy_B1(sy):
                for i in range(4):
                    sy.wait_ge(sV.h, v_xln(LAST_A_XF[i]))
                    sy.wait_ge(sA.h, A_SSQ[LAST_A_XF[i]])
                    sy.dma_start(out=xf[i][:, :],
                                 in_=x_d.ap()[128 * i:128 * (i + 1), :]
                                 ).then_inc(sXF[i].h, 16)
                    sXF[i].n += 16
                for i in range(NT):
                    if i >= 1:
                        sy.wait_ge(sV.h, v_x1(i - 1))
                        sy.dma_start(out=x1_d.ap()[128 * (i - 1):128 * i, :],
                                     in_=x1sb[(i - 1) % 2][:, :]
                                     ).then_inc(sO1[(i - 1) % 2].h, 16)
                        sO1[(i - 1) % 2].n += 16
                        if i + 3 < NT:
                            sy.dma_start(
                                out=xf[(i + 3) % 4][:, :],
                                in_=x_d.ap()[128 * (i + 3):128 * (i + 4), :]
                                ).then_inc(sXF[(i + 3) % 4].h, 16)
                            sXF[(i + 3) % 4].n += 16
                sy.wait_ge(sV.h, v_x1(NT - 1))
                sy.dma_start(out=x1_d.ap()[128 * (NT - 1):128 * NT, :],
                             in_=x1sb[(NT - 1) % 2][:, :]
                             ).then_inc(sO1[(NT - 1) % 2].h, 16)
                sO1[(NT - 1) % 2].n += 16

            blockB.sync(sy_B1)

            def v_B1(v):
                zbuf = [z_sc, z_sc2]
                zv = {}
                for i in range(NT + 1):
                    if i < NT:
                        if i >= 2:
                            v.wait_ge(sVC.h, zv[i - 2] + 1)
                        v.tensor_tensor(zbuf[i % 2][:, :],
                                        qf_all[:, D * i:D * (i + 1)],
                                        ksb[:, :], ALU.mult
                                        ).then_inc(sVC.h, 1)
                        sVC.n += 1
                        zv[i] = sVC.n
                    if i >= 1:
                        j = i - 1
                        v.wait_ge(sVC.h, zv[j])
                        v.reduce_sum(
                            den_all[:, NH * j:NH * (j + 1)],
                            zbuf[j % 2][:].rearrange("p (h d) -> p h d", d=HD),
                            axis=AX.X).then_inc(sVC.h, 1)
                        sVC.n += 1
                v.wait_ge(sVC.h, sVC.n)
                vch(v, v.tensor_scalar_max(den_all[:, :], den_all[:, :], 1e-6))
                v.reciprocal(rden_all[:, :], den_all[:, :]).then_inc(sV.h, 1)
                sV.n += 1
                assert sV.n == V_DEN
                def clamp1(k):
                    qq, dc = divmod(k, 8)
                    v.wait_ge(sBG[k % 2].h, BG(k))
                    sl = bw1[dc][:, 1024 * qq:1024 * (qq + 1)]
                    vch(v, v.tensor_scalar(z_sc[:, :], binf2[k % 2][:, :],
                                           1e38, 1.0, ALU.mult, ALU.min))
                    v.tensor_scalar_max(sl, z_sc[:, :], -1.0
                                        ).then_inc(sV.h, 1)
                    sV.n += 1
                    assert sV.n == VL_BIN1[k]

                def qfdT_evac(i):
                    tt = i % GNT
                    v.wait_ge(sM.h, m_tp1(i, DC - 1))
                    if i // GNT >= 1:
                        v.wait_ge(sM.h, m_att(i // GNT - 1, 7))
                    for bk in range(2):
                        v.tensor_copy(
                            ap(qfdT, 256 * bk + 128 * tt,
                               [[2048, 128], [512, 4], [1, 128]]),
                            ap(tp2[bk], 0, [[512, 128], [128, 4], [1, 128]])
                            ).then_inc(sTB2[i % 2].h, 1)
                        sTB2[i % 2].n += 1

                for g in range(NG):
                    for tt in range(GNT):
                        i = GNT * g + tt
                        if i >= 2:
                            v.wait_ge(sM.h, m_tp1(i - 2, DC - 1))
                        v.tensor_tensor(
                            qfd[i % 2][:].rearrange("p (h d) -> p h d", d=HD),
                            qf_all[:, D * i:D * (i + 1)].rearrange(
                                "p (h d) -> p h d", d=HD),
                            ap(rden_all, NH * i, [[NT * NH, 128], [1, 16],
                                                  [0, 64]]),
                            ALU.mult).then_inc(sV.h, 1)
                        sV.n += 1
                        assert sV.n == v_qfd(i)
                    clamp1(4 * g)
                    qfdT_evac(GNT * g)
                    clamp1(4 * g + 1)
                    qfdT_evac(GNT * g + 1)
                    for tt in range(GNT):
                        i = GNT * g + tt
                        v.wait_ge(sM.h, m_op(g, tt))
                        v.wait_ge(sXF[i % 4].h, XFB(i))
                        if i >= 2:
                            v.wait_ge(sO1[i % 2].h, SO1(i - 2))
                        v.tensor_add(x1sb[i % 2][:, :], op_ps[:, :],
                                     xf[i % 4][:, :]).then_inc(sV.h, 1)
                        sV.n += 1
                        assert sV.n == v_x1(i)
                    clamp1(4 * g + 2)
                    clamp1(4 * g + 3)
                v.wait_ge(sA.h, A_ABS[63])
                v.tensor_scalar_mul(a1n[:, :], a1n[:, :], 1.0 / D)
                vch(v, v.reduce_sum(a2n[:, :],
                                    a2n4[:].rearrange("p (r s) -> p r s", s=4),
                                    axis=AX.X))
                v.tensor_scalar_mul(a2n[:, :], a2n[:, :], 1.0 / FH
                                    ).then_inc(sV.h, 1)
                sV.n += 1
                assert sV.n == V_A1F

            blockB.vector(v_B1)

            def s_B1(s):
                for g in range(NG):
                    s.wait_ge(sM.h, m_att(g, DC - 1))
                    for c in range(DC):
                        s.activation(attT[:, GT * c:GT * (c + 1)],
                                     att_ps[c // 2][:, 256 * (c % 2):
                                                    256 * (c % 2) + 256],
                                     AF.Copy).then_inc(sA.h, 1)
                        sA.n += 1
                assert sA.n == A_END_B1

            blockB.scalar(s_B1)

            def t_B1(t):
                t.wait_ge(sG.h, G_KV2)
                for g in range(NG):
                    for tt in range(GNT):
                        i = GNT * g + tt
                        t.wait_ge(sV.h, v_qfd(i))
                        if i >= 1:
                            t.wait_ge(sTB2[(i - 1) % 2].h, TBQ(i - 1))
                        for c in range(DC):
                            t.transpose(tp2[c % 2][:, 128 * (c // 2):
                                                   128 * (c // 2) + 128],
                                        qfd[i % 2][:, 128 * c:128 * (c + 1)],
                                        ident[:, :]).then_inc(sM.h, 1)
                            sM.n += 1
                            assert sM.n == m_tp1(i, c)
                    t.wait_ge(sTB2[0].h, TBQ(GNT * g))
                    t.wait_ge(sTB2[1].h, TBQ(GNT * g + 1))
                    if g >= 1:
                        t.wait_ge(sA.h, a_att(g - 1, DC - 1))
                    for c in range(DC):
                        t.matmul(att_ps[c // 2][0:64, 256 * (c % 2):
                                                256 * (c % 2) + 256],
                                 kv2[0:64, 64 * c:64 * (c + 1)],
                                 qfdT[0:64, GT * c:GT * (c + 1)],
                                 start=True, stop=True, tile_position=(0, 0),
                                 skip_group_check=True)
                        last = t.matmul(att_ps[c // 2][64:128, 256 * (c % 2):
                                                       256 * (c % 2) + 256],
                                        kv2[64:128, 64 * c:64 * (c + 1)],
                                        qfdT[64:128, GT * c:GT * (c + 1)],
                                        start=True, stop=True,
                                        tile_position=(64, 64),
                                        skip_group_check=True)
                        last.then_inc(sM.h, 1)
                        sM.n += 1
                        assert sM.n == m_att(g, c)
                    if g == 0:
                        t.wait_ge(sW.h, 256)
                    t.wait_ge(sA.h, a_att(g, DC - 1))
                    for tt in range(GNT):
                        i = GNT * g + tt
                        if i >= 1:
                            t.wait_ge(sV.h, v_x1(i - 1))
                        last = None
                        for c in range(DC):
                            for n in range(2):
                                last = t.matmul(
                                    op_ps[:, 512 * n:512 * (n + 1)],
                                    attT[:, GT * c + 128 * tt:
                                         GT * c + 128 * (tt + 1)],
                                    wo_sb[c][:, 512 * n:512 * (n + 1)],
                                    start=(c == 0), stop=(c == DC - 1),
                                    skip_group_check=True)
                        last.then_inc(sM.h, 1)
                        sM.n += 1
                        assert sM.n == m_op(g, tt)
                assert sM.n == M_END_B1

            blockB.tensor(t_B1)

            if debug:
                def g_dbg(g):
                    g.wait_ge(sV.h, v_xln(0))
                    g.dma_start(out=dbg_xln.ap(), in_=xln_b[0][:, :]
                                ).then_inc(sD.h, 16)
                    g.wait_ge(sV.h, v_qf(0))
                    g.dma_start(out=dbg_qf.ap(), in_=qf_all[:, 0:D]
                                ).then_inc(sD.h, 16)
                    g.wait_ge(sV.h, v_kf(0))
                    g.dma_start(out=dbg_kf.ap(), in_=kf_b[:, :]).then_inc(sD.h, 16)
                    g.wait_ge(sC.h, 1)
                    g.dma_start(out=dbg_cc.ap(), in_=cco_d.ap()).then_inc(sD.h, 16)
                    g.wait_ge(sV.h, v_qfd(0))
                    g.dma_start(out=dbg_den.ap(), in_=den[:, :]).then_inc(sD.h, 16)
                    g.wait_ge(sA.h, a_att(0, DC - 1))
                    g.dma_start(out=dbg_att.ap(), in_=attT[:, :]).then_inc(sD.h, 16)
                    g.wait_ge(sV.h, v_x1(0))
                    g.dma_start(out=dbg_x1.ap(), in_=x1sb[0][:, :]
                                ).then_inc(sD.h, 16)
                    g.wait_ge(sD.h, 112)
                blockB.gpsimd(g_dbg)

        if phases < 3:
            return nc
        ab1.close()
        w2b = [root.enter_context(nc.sbuf_tensor(f"w2b_{c}", [128, D], BF16,
                                                 side="right"))
               for c in range(NHC)]
        xl2T = [root.enter_context(nc.sbuf_tensor(f"xl2T{j}", [128, DC * GT],
                                                  BF16)) for j in range(2)]
        hTb = root.enter_context(nc.sbuf_tensor("hTb", [128, NHC * GT], BF16))
        outsb = [root.enter_context(nc.sbuf_tensor(f"outsb{j}", [128, D], F32,
                                                   side="right"))
                 for j in range(2)]
        xln_b2 = [root.enter_context(nc.sbuf_tensor(f"xln2_{j}", [128, D], BF16,
                                                    side="right"))
                  for j in range(2)]
        h_ps = [root.enter_context(nc.psum_tensor(f"hps{j}", [128, GT], F32))
                for j in range(2)]
        o2_ps = [root.enter_context(nc.psum_tensor(f"o2ps{j}", [128, D], F32))
                 for j in range(2)]

        def g_B2(g):
            for k in range(NHC):
                if k < 2:
                    g.wait_ge(sV.h, VL_BIN1[30 + k])
                else:
                    g.wait_ge(sV.h, VL_BIN2[k - 2])
                g.dma_start(out=binf2[k % 2][:, :],
                            in_=w2t_d.ap()[128 * k:128 * (k + 1), :]
                            ).then_inc(sBG[k % 2].h, 16)
                sBG[k % 2].n += 16

        blockB.gpsimd(g_B2)

        def s_B2(s):
            s.wait_ge(sV.h, V_A1F)
            for g in range(NG):
                for tt in range(GNT):
                    i = GNT * g + tt
                    s.wait_ge(sXF[i % 4].h, XFG(i))
                    if i >= 1:
                        s.wait_ge(sV.h, VL_V2VAR[i - 1])
                    s.activation(junk[:, :], xf[i % 4][:, :], AF.Square,
                                 accum_out=st_ssq[:, :]).then_inc(sA.h, 1)
                    sA.n += 1
                    s.wait_ge(sV.h, VL_V2VAR[i])
                    s.activation(st_std[:, :], st_var[:, :], AF.Sqrt
                                 ).then_inc(sA.h, 1)
                    sA.n += 1
                for tt in range(GNT):
                    i = GNT * g + tt
                    s.wait_ge(sV.h, VL_V2XLN[i])
                    if g >= 2 and tt == 0:
                        s.wait_ge(sM.h, m_ffn1(g - 2, NHC - 1))
                    for c in range(DC):
                        s.dma_start(
                            out=xl2T[g % 2][:, GT * c + 128 * tt:
                                            GT * c + 128 * (tt + 1)],
                            in_=xln_b2[i % 2][:, 128 * c:128 * (c + 1)],
                            transpose=True).then_inc(sTB2[i % 2].h, 16)
                    sTB2[i % 2].n += 128
                for hc in range(NHC):
                    s.wait_ge(sM.h, m_ffn1(g, hc))
                    s.activation(hTb[:, GT * hc:GT * (hc + 1)],
                                 h_ps[hc % 2][:, :], AF.Gelu,
                                 bias=b1T[:, hc:hc + 1],
                                 scale=a1n[:, hc:hc + 1]).then_inc(sA.h, 1)
                    sA.n += 1
            assert sA.n == A_FINAL

        blockB.scalar(s_B2)

        def v_B2(v):
            def ln2(i):
                xt = xf[i % 4]
                v.wait_ge(sXF[i % 4].h, XFG(i))
                vch(v, v.reduce_sum(st_sum[:, :], xt[:, :], axis=AX.X))
                vch(v, v.tensor_scalar_mul(st_nm[:, :], st_sum[:, :], -1.0 / D))
                v.wait_ge(sA.h, SA2_SSQ[i])
                v.tensor_scalar_mul(st_v1[:, :], st_ssq[:, :], 1.0 / D)
                vch(v, v.tensor_mul(st_nm2[:, :], st_nm[:, :], st_nm[:, :]))
                v.scalar_tensor_tensor(st_var[:, :], st_v1[:, :], 1e-5,
                                       st_nm2[:, :], ALU.add, ALU.subtract
                                       ).then_inc(sV.h, 1)
                sV.n += 1
                assert sV.n == VL_V2VAR[i]
                v.wait_ge(sA.h, SA2_STD[i])
                v.reciprocal(st_rstd[:, :], st_std[:, :]).then_inc(sV.h, 1)
                sV.n += 1
                v.wait_ge(sV.h, VL_V2RSTD[i])
                if i >= 2:
                    v.wait_ge(sTB2[i % 2].h, TBX(i - 2))
                vch(v, v.tensor_scalar_add(z_sc[:, :], xt[:, :], st_nm[:, :]))
                v.tensor_scalar_mul(xln_b2[i % 2][:, :], z_sc[:, :],
                                    st_rstd[:, :]).then_inc(sV.h, 1)
                sV.n += 1
                assert sV.n == VL_V2XLN[i]

            def x2(i, g, tt):
                v.wait_ge(sM.h, m_ffn2(g, tt))
                if i >= 2:
                    v.wait_ge(sO2[i % 2].h, SO2(i - 2))
                v.tensor_add(outsb[i % 2][:, :], o2_ps[i % 2][:, :],
                             xf[i % 4][:, :]).then_inc(sV.h, 1)
                sV.n += 1
                assert sV.n == VL_V2X2[i]

            v.wait_ge(sM.h, M_B2BC)
            v.tensor_copy(arep2[:, :], o2_ps[0][:, :]).then_inc(sV.h, 1)
            sV.n += 1
            assert sV.n == V_AREP2
            v.wait_ge(sV.h, sV.n)
            for gg in range(2):
                for tt in range(GNT):
                    ln2(GNT * gg + tt)
            for k in range(NHC):
                v.wait_ge(sBG[k % 2].h, BG(32 + k))
                vch(v, v.tensor_scalar(z_sc[:, :], binf2[k % 2][:, :], 1e38,
                                       1.0, ALU.mult, ALU.min))
                v.tensor_scalar_max(w2b[k][:, :], z_sc[:, :], -1.0
                                    ).then_inc(sV.h, 1)
                sV.n += 1
                assert sV.n == VL_BIN2[k]
            for hc in range(NHC):
                v.tensor_tensor(w2b[hc][:, :], w2b[hc][:, :], arep2[:, :],
                                ALU.mult).then_inc(sV.h, 1)
                sV.n += 1
                assert sV.n == VL_M2[hc]
            for g in range(NG):
                for tt in range(GNT):
                    x2(GNT * g + tt, g, tt)
                if g + 2 < NG:
                    for tt in range(GNT):
                        ln2(GNT * (g + 2) + tt)
            assert sV.n == V_FINAL

        blockB.vector(v_B2)

        def sy_B2(sy):
            def load(i):
                sy.wait_ge(sO1[i % 2].h, SO1(i))
                if i >= 4:
                    sy.wait_ge(sV.h, VL_V2X2[i - 4])
                    sy.wait_ge(sA.h, SA2_SSQ[i - 4])
                else:
                    sy.wait_ge(sV.h, v_x1(LAST_A_XF[i]))
                sy.dma_start(out=xf[i % 4][:, :],
                             in_=x1_d.ap()[128 * i:128 * (i + 1), :]
                             ).then_inc(sXF[i % 4].h, 16)
                sXF[i % 4].n += 16

            for i in range(2 * GNT):
                load(i)
            for g in range(NG):
                for tt in range(GNT):
                    i = GNT * g + tt
                    sy.wait_ge(sV.h, VL_V2X2[i])
                    sy.dma_start(out=out_d.ap()[128 * i:128 * (i + 1), :],
                                 in_=outsb[i % 2][:, :]).then_inc(sO2[i % 2].h,
                                                                  16)
                    sO2[i % 2].n += 16
                if g + 2 < NG:
                    for tt in range(GNT):
                        load(GNT * (g + 2) + tt)
            sy.wait_ge(sO2[0].h, 128)
            sy.wait_ge(sO2[1].h, 128)

        blockB.sync(sy_B2)

        def t_B2(t):
            t.wait_ge(sV.h, v_x1(NT - 1))
            t.wait_ge(sG.h, G_A2R)
            last = None
            for n in range(2):
                last = t.matmul(o2_ps[0][:, 512 * n:512 * (n + 1)],
                                onesf[0:1, :],
                                a2row[0:1, 512 * n:512 * (n + 1)],
                                start=True, stop=True, skip_group_check=True)
            last.then_inc(sM.h, 1)
            sM.n += 1
            assert sM.n == M_B2BC
            for g in range(NG):
                t.wait_ge(sTB2[0].h, TBX(GNT * g))
                t.wait_ge(sTB2[1].h, TBX(GNT * g + 1))
                for hc in range(NHC):
                    if g == 0:
                        t.wait_ge(sV.h, VL_BIN1[8 * (hc // 8) + 7])
                    if hc >= 2:
                        t.wait_ge(sA.h, SA2_GELU[(g, hc - 2)])
                    elif g >= 1:
                        t.wait_ge(sA.h, SA2_GELU[(g - 1, NHC - 2 + hc)])
                    last = None
                    for dc in range(DC):
                        last = t.matmul(h_ps[hc % 2][:, :],
                                        bw1[dc][:, 128 * hc:128 * (hc + 1)],
                                        xl2T[g % 2][:, GT * dc:GT * (dc + 1)],
                                        start=(dc == 0), stop=(dc == DC - 1),
                                        skip_group_check=True)
                    last.then_inc(sM.h, 1)
                    sM.n += 1
                t.wait_ge(sA.h, SA2_GELU[(g, NHC - 1)])
                for tt in range(GNT):
                    i = GNT * g + tt
                    if i >= 2:
                        t.wait_ge(sV.h, VL_V2X2[i - 2])
                    for hc in range(NHC):
                        t.wait_ge(sV.h, VL_M2[hc])
                        for n in range(2):
                            t.matmul(o2_ps[i % 2][:, 512 * n:512 * (n + 1)],
                                     hTb[:, GT * hc + 128 * tt:
                                         GT * hc + 128 * (tt + 1)],
                                     w2b[hc][:, 512 * n:512 * (n + 1)],
                                     start=(hc == 0), stop=False,
                                     skip_group_check=True)
                    last = None
                    for n in range(2):
                        last = t.matmul(
                            o2_ps[i % 2][:, 512 * n:512 * (n + 1)],
                            onesr[0:1, :],
                            b2b[0:1, 512 * n:512 * (n + 1)],
                            start=False, stop=True, skip_group_check=True)
                    last.then_inc(sM.h, 1)
                    sM.n += 1

        blockB.tensor(t_B2)

    return nc


_NC_CACHE = {}


def _get_nc():
    if "nc" not in _NC_CACHE:
        _NC_CACHE["nc"] = build(debug=False)
    return _NC_CACHE["nc"]


def kernel(L0_attn_x, L0_attn_w_qkv, L0_attn_w_out, L0_ffn_w1, L0_ffn_b1,
           L0_ffn_w2, L0_ffn_b2):
    x = np.ascontiguousarray(np.asarray(L0_attn_x, dtype=np.float32))
    wq = np.asarray(L0_attn_w_qkv, dtype=np.float32)
    wo = np.asarray(L0_attn_w_out, dtype=np.float32)
    w1 = np.ascontiguousarray(np.asarray(L0_ffn_w1, dtype=np.float32))
    b1 = np.asarray(L0_ffn_b1, dtype=np.float32)
    w2 = np.ascontiguousarray(np.asarray(L0_ffn_w2, dtype=np.float32))
    b2 = np.asarray(L0_ffn_b2, dtype=np.float32)

    wqT = np.ascontiguousarray(wq.T)
    woT = np.ascontiguousarray(wo.T)
    w1T = np.ascontiguousarray(w1.T)
    w2T = np.ascontiguousarray(w2.T)
    b1r = np.ascontiguousarray(b1.reshape(1, FH))
    b2r = np.ascontiguousarray(b2.reshape(1, D))

    nc = _get_nc()
    in_maps = []
    for c in range(8):
        b, q = divmod(c, 4)
        in_maps.append({
            "x": np.ascontiguousarray(x[b, T * q:T * (q + 1), :]),
            "wqkvT": wqT, "woutT": woT, "w1T": w1T, "w2T": w2T,
            "w1n": w1, "w2n": w2, "b1": b1r, "b2": b2r,
            "tbase": np.full((128, 1), float(T * q), np.float32),
        })
    res = run_bass_kernel_spmd(nc, in_maps, core_ids=list(range(8)))
    out = np.empty((2, 8192, 1024), np.float32)
    for c in range(8):
        b, q = divmod(c, 4)
        out[b, T * q:T * (q + 1), :] = res.results[c]["out"]
    return out


# version 10
# speedup vs baseline: 1.0337x; 1.0337x over previous
import numpy as np
from contextlib import ExitStack

import concourse.bass as bass
import concourse.mybir as mybir
from concourse.bass_utils import run_bass_kernel_spmd

F32 = mybir.dt.float32
BF16 = mybir.dt.bfloat16
I32 = mybir.dt.int32
AF = mybir.ActivationFunctionType
ALU = mybir.AluOpType
AX = mybir.AxisListType

D = 1024
T = 2048
NT = 16
NH = 16
HD = 64
DC = 8
FH = 4096
NHC = 32
GT = 256
NG = T // GT
GNT = GT // 128

TWO_PI = 2.0 * np.pi
CW1 = 6.28125
CW2 = float(np.float32(TWO_PI - CW1))
CW3 = float(TWO_PI - CW1 - np.float64(np.float32(TWO_PI - CW1)))
LN1E4 = float(np.log(10000.0))
QSCALE = float(HD ** -0.5)

ABS_SCHED = [0, 0, 6, 6, 5, 5] + [4] * 9 + [6]
assert sum(ABS_SCHED) == 64 and len(ABS_SCHED) == NT


def ap(t, off, dims):
    h = t.handle if hasattr(t, "handle") else t
    return bass.AP(h, off, dims)


class Sem:
    def __init__(self, ctx, nc, name):
        self.h = ctx.enter_context(nc.semaphore(name))
        self.n = 0


def build(debug=False, dump=False, phases=3):
    nc = bass.Bass("TRN2", target_bir_lowering=False, debug=False,
                   detect_race_conditions=False)

    x_d = nc.dram_tensor("x", [T, D], F32, kind="ExternalInput")
    wq_d = nc.dram_tensor("wqkvT", [D, 3 * D], F32, kind="ExternalInput")
    wo_d = nc.dram_tensor("woutT", [D, D], F32, kind="ExternalInput")
    w1t_d = nc.dram_tensor("w1T", [D, FH], F32, kind="ExternalInput")
    w2t_d = nc.dram_tensor("w2T", [FH, D], F32, kind="ExternalInput")
    w1n_d = nc.dram_tensor("w1n", [FH, D], F32, kind="ExternalInput")
    w2n_d = nc.dram_tensor("w2n", [D, FH], F32, kind="ExternalInput")
    b1_d = nc.dram_tensor("b1", [1, FH], F32, kind="ExternalInput")
    b2_d = nc.dram_tensor("b2", [1, D], F32, kind="ExternalInput")
    tb_d = nc.dram_tensor("tbase", [128, 1], F32, kind="ExternalInput")
    out_d = nc.dram_tensor("out", [T, D], F32, kind="ExternalOutput")
    x1_d = nc.dram_tensor("x1s", [T, D], F32,
                          kind="ExternalOutput" if dump else "Internal")
    a1_d = nc.dram_tensor("a1s", [1, FH], F32)
    a2_d = nc.dram_tensor("a2s", [1, D], F32)
    cci_d = nc.dram_tensor("cci", [65, D], F32)
    cco_d = nc.dram_tensor("cco", [65, D], F32)
    if dump:
        dmp_cc = nc.dram_tensor("dmp_cc", [65, D], F32, kind="ExternalOutput")
        dmp_qf = nc.dram_tensor("dmp_qf", [128, NT * D], F32,
                                kind="ExternalOutput")
        dmp_xln = nc.dram_tensor("dmp_xln", [128, D], F32, kind="ExternalOutput")
        dmp_q = nc.dram_tensor("dmp_q", [128, D], F32, kind="ExternalOutput")
        dmp_st = nc.dram_tensor("dmp_st", [128, 8], F32, kind="ExternalOutput")
    if debug:
        dbg_xln = nc.dram_tensor("dbg_xln", [128, D], F32, kind="ExternalOutput")
        dbg_qf = nc.dram_tensor("dbg_qf", [128, D], F32, kind="ExternalOutput")
        dbg_kf = nc.dram_tensor("dbg_kf", [128, D], F32, kind="ExternalOutput")
        dbg_cc = nc.dram_tensor("dbg_cc", [65, D], F32, kind="ExternalOutput")
        dbg_den = nc.dram_tensor("dbg_den", [128, NH], F32, kind="ExternalOutput")
        dbg_att = nc.dram_tensor("dbg_att", [128, DC * GT], F32, kind="ExternalOutput")
        dbg_x1 = nc.dram_tensor("dbg_x1", [128, D], F32, kind="ExternalOutput")

    with ExitStack() as root:
        sW = Sem(root, nc, "sW")
        sXF = [Sem(root, nc, f"sXF{j}") for j in range(4)]
        sT2 = [Sem(root, nc, f"sT2{j}") for j in range(2)]
        sTB2 = [Sem(root, nc, f"sTB2{j}") for j in range(2)]
        sBF = [Sem(root, nc, f"sBF{j}") for j in range(2)]
        sBG = [Sem(root, nc, f"sBG{j}") for j in range(2)]
        sV = Sem(root, nc, "sV")
        sA = Sem(root, nc, "sA")
        sM = Sem(root, nc, "sM")
        sG = Sem(root, nc, "sG")
        sC = Sem(root, nc, "sC")
        sB = Sem(root, nc, "sB")
        sO1 = [Sem(root, nc, f"sO1{j}") for j in range(2)]
        sO2 = [Sem(root, nc, f"sO2{j}") for j in range(2)]
        sD = Sem(root, nc, "sD")
        sVC = Sem(root, nc, "sVC")

        def vch(v, inst):
            inst.then_inc(sVC.h, 1)
            sVC.n += 1
            v.wait_ge(sVC.h, sVC.n)

        tb_sb = root.enter_context(nc.sbuf_tensor("tb", [128, 1], F32))
        b1T = root.enter_context(nc.sbuf_tensor("b1T", [128, NHC], F32))
        b2b = root.enter_context(nc.sbuf_tensor("b2b", [1, D], BF16))
        onesr = root.enter_context(nc.sbuf_tensor("onesr", [1, 128], BF16))
        onesc = root.enter_context(nc.sbuf_tensor("onesc", [128, 1], BF16))
        onesf = root.enter_context(nc.sbuf_tensor("onesf", [1, 128], F32))
        ident = root.enter_context(nc.sbuf_tensor("ident", [128, 128], BF16))
        io1 = root.enter_context(nc.sbuf_tensor("io1", [128, 128], F32))
        io2 = root.enter_context(nc.sbuf_tensor("io2", [128, 128], F32))
        zer65 = root.enter_context(nc.sbuf_tensor("zer65", [128, 65], BF16))
        a2row = root.enter_context(nc.sbuf_tensor("a2row", [1, D], F32))
        a1n = root.enter_context(nc.sbuf_tensor("a1n", [128, 32], F32))
        a2n4 = root.enter_context(nc.sbuf_tensor("a2n4", [128, 32], F32))
        a2n = root.enter_context(nc.sbuf_tensor("a2n", [128, 8], F32))
        arep2 = root.enter_context(nc.sbuf_tensor("arep2", [128, D], BF16))
        xf = [root.enter_context(nc.sbuf_tensor(f"xf{j}", [128, D], F32))
              for j in range(4)]
        st_sum = root.enter_context(nc.sbuf_tensor("st_sum", [128, 1], F32))
        st_nm = root.enter_context(nc.sbuf_tensor("st_nm", [128, 1], F32))
        st_ssq = root.enter_context(nc.sbuf_tensor("st_ssq", [128, 1], F32))
        st_v1 = root.enter_context(nc.sbuf_tensor("st_v1", [128, 1], F32))
        st_nm2 = root.enter_context(nc.sbuf_tensor("st_nm2", [128, 1], F32))
        st_var = root.enter_context(nc.sbuf_tensor("st_var", [128, 1], F32))
        st_std = root.enter_context(nc.sbuf_tensor("st_std", [128, 1], F32))
        st_rstd = root.enter_context(nc.sbuf_tensor("st_rstd", [128, 1], F32))
        den = root.enter_context(nc.sbuf_tensor("den", [128, NH], F32))
        rden = root.enter_context(nc.sbuf_tensor("rden", [128, NH], F32))
        junk = root.enter_context(nc.sbuf_tensor("junk", [128, D], F32))
        z_sc = root.enter_context(nc.sbuf_tensor("z_sc", [128, D], F32))
        if dump:
            dx_sb = root.enter_context(nc.sbuf_tensor("dx_sb", [128, D], BF16))
            dq_sb = root.enter_context(nc.sbuf_tensor("dq_sb", [128, D], BF16))
            ds_sb = root.enter_context(nc.sbuf_tensor("ds_sb", [128, 8], F32))

        B_PREP = 9
        G_PREP = 48
        G_KVB = 80
        G_KV2 = 112
        G_KSR = 128
        G_A2B = 144
        G_A2R = 160
        XFA = lambda i: 16 * (i // 4 + 1)
        XFB = lambda i: 16 * (i // 4 + 5)
        XFG = lambda i: 16 * (i // 4 + 9)
        STx = lambda i: 2 * (i // 2 + 1)
        TBQ = lambda i: 2 * (i // 2 + 1)
        TBX = lambda i: 16 + 128 * (i // 2 + 1)
        BF = lambda j: 16 * (j // 2 + 1)
        BG = lambda k: 16 * (k // 2 + 1)
        SO1 = lambda i: 16 * (i // 2 + 1)
        SO2 = lambda i: 16 * (i // 2 + 1)

        A_INVF, A_SIN, A_COS = 1, 2, 3
        A_SSQ, A_STD, A_EVQ, A_EVK = {}, {}, {}, {}
        A_EQ, A_VB, A_EK, A_ABS = {}, {}, {}, {}
        cnt = 3
        absj = 0
        for i in range(NT + 2):
            if i == 0:
                for k in range(3):
                    A_SSQ[k] = cnt + 1
                    A_STD[k] = cnt + 2
                    cnt += 2
            if i >= 2:
                jj = i - 2
                A_EQ[jj] = cnt + 1
                A_EK[jj] = cnt + 2
                A_VB[jj] = cnt + 3
                cnt += 3
            if 1 <= i <= NT:
                A_EVQ[i - 1] = cnt + 1
                A_EVK[i - 1] = cnt + 2
                cnt += 2
            if i >= 1 and i + 2 < NT:
                A_SSQ[i + 2] = cnt + 1
                A_STD[i + 2] = cnt + 2
                cnt += 2
            if i >= 2:
                for _ in range(ABS_SCHED[i - 2]):
                    cnt += 1
                    A_ABS[absj] = cnt
                    absj += 1
        A_END_A = cnt
        assert absj == 64
        a_att = lambda g, c: A_END_A + 8 * g + c + 1
        A_END_B1 = A_END_A + 8 * NG
        SA2_SSQ, SA2_STD, SA2_GELU = {}, {}, {}
        cnt = A_END_B1
        for g in range(NG):
            for tt in range(GNT):
                i = GNT * g + tt
                SA2_SSQ[i] = cnt + 1
                SA2_STD[i] = cnt + 2
                cnt += 2
            for hc in range(NHC):
                cnt += 1
                SA2_GELU[(g, hc)] = cnt
        A_FINAL = cnt

        V_PREP = 4
        V_VARd, V_RSTDd, V_XLNd = {}, {}, {}
        V_ROPEQd, V_QFd, V_ROPEKd, V_KFd = {}, {}, {}, {}
        cnt = V_PREP
        for i in range(NT + 2):
            if i == 0:
                for k in range(2):
                    V_VARd[k] = cnt + 1
                    V_RSTDd[k] = cnt + 2
                    V_XLNd[k] = cnt + 3
                    cnt += 3
            if i >= 2:
                jj = i - 2
                V_ROPEQd[jj] = cnt + 1
                V_ROPEKd[jj] = cnt + 2
                V_QFd[jj] = cnt + 3
                V_KFd[jj] = cnt + 4
                cnt += 4
            if i + 2 < NT:
                V_VARd[i + 2] = cnt + 1
                V_RSTDd[i + 2] = cnt + 2
                V_XLNd[i + 2] = cnt + 3
                cnt += 3
        v_var = lambda i: V_VARd[i]
        v_rstd = lambda i: V_RSTDd[i]
        v_xln = lambda i: V_XLNd[i]
        v_ropeq = lambda i: V_ROPEQd[i]
        v_qf = lambda i: V_QFd[i]
        v_ropek = lambda i: V_ROPEKd[i]
        v_kf = lambda i: V_KFd[i]
        V_KVEV = cnt + 1
        V_KSB = V_KVEV + 1
        V_DEN = V_KSB + 1
        VL_QFD, VL_X1, VL_BIN1 = {}, {}, {}
        cur = V_DEN
        for g in range(NG):
            for tt in range(GNT):
                cur += 1
                VL_QFD[GNT * g + tt] = cur
            for kk in range(2):
                cur += 1
                VL_BIN1[4 * g + kk] = cur
            for tt in range(GNT):
                cur += 1
                VL_X1[GNT * g + tt] = cur
            for kk in range(2, 4):
                cur += 1
                VL_BIN1[4 * g + kk] = cur
        v_qfd = lambda i: VL_QFD[i]
        v_x1 = lambda i: VL_X1[i]
        V_A1F = cur + 1
        V_AREP2 = V_A1F + 1
        VL_BIN2, VL_M2 = {}, {}
        VL_V2VAR, VL_V2RSTD, VL_V2XLN, VL_V2X2 = {}, {}, {}, {}
        cur = V_AREP2

        def _ln2p(i):
            global_ = None
            return i

        for gg in range(2):
            for tt in range(GNT):
                i = GNT * gg + tt
                cur += 3
                VL_V2VAR[i] = cur - 2
                VL_V2RSTD[i] = cur - 1
                VL_V2XLN[i] = cur
        for k in range(NHC):
            cur += 1
            VL_BIN2[k] = cur
        for hc in range(NHC):
            cur += 1
            VL_M2[hc] = cur
        for g in range(NG):
            for tt in range(GNT):
                i = GNT * g + tt
                cur += 1
                VL_V2X2[i] = cur
            if g + 2 < NG:
                for tt in range(GNT):
                    i = GNT * (g + 2) + tt
                    cur += 3
                    VL_V2VAR[i] = cur - 2
                    VL_V2RSTD[i] = cur - 1
                    VL_V2XLN[i] = cur
        V_FINAL = cur

        M_Qd, M_Kd, M_Vd, M_KVd, M_KSd, M_TPA = {}, {}, {}, {}, {}, {}
        cnt = 0
        for j in range(NT):
            for c in range(DC):
                cnt += 1
                M_TPA[(j, c)] = cnt
            M_Qd[j] = cnt + 1
            M_Kd[j] = cnt + 2
            cnt += 2
            if j >= 1:
                M_KVd[j - 1] = cnt + 1
                M_KSd[j - 1] = cnt + 2
                cnt += 2
            M_Vd[j] = cnt + 1
            cnt += 1
        M_KVd[NT - 1] = cnt + 1
        M_KSd[NT - 1] = cnt + 2
        cnt += 2
        M_END_A = cnt + 1
        m_qp = lambda i: M_Qd[i]
        m_kp = lambda i: M_Kd[i]
        m_vp = lambda i: M_Vd[i]
        m_tp1 = lambda i, c: (M_END_A + (16 + 8 + GNT) * (i // GNT)
                              + 8 * (i % GNT) + c + 1)
        m_att = lambda g, c: M_END_A + (16 + 8 + GNT) * g + 16 + c + 1
        m_op = lambda g, tt: M_END_A + (16 + 8 + GNT) * g + 24 + tt + 1
        M_END_B1 = M_END_A + (16 + 8 + GNT) * NG
        M_B2BC = M_END_B1 + 1
        m_ffn1 = lambda g, hc: M_B2BC + (NHC + GNT) * g + hc + 1
        m_ffn2 = lambda g, tt: M_B2BC + (NHC + GNT) * g + NHC + tt + 1

        LAST_A_XF = {0: 12, 1: 13, 2: 14, 3: 15}

        ab1 = root.enter_context(ExitStack())
        qf_all = ab1.enter_context(nc.sbuf_tensor("qf_all", [128, NT * D], BF16,
                                                  side="right"))
        wo_sb = [ab1.enter_context(nc.sbuf_tensor(f"wo{c}", [128, D], BF16,
                                                  side="right"))
                 for c in range(DC)]
        kv2 = ab1.enter_context(nc.sbuf_tensor("kv2", [128, 8 * HD], BF16,
                                               side="right"))
        ksb = ab1.enter_context(nc.sbuf_tensor("ksb", [128, D], F32,
                                               side="right"))
        binf = [ab1.enter_context(nc.sbuf_tensor(f"binf{j}", [128, D], F32,
                                                 side="right"))
                for j in range(2)]
        ksrow = ab1.enter_context(nc.sbuf_tensor("ksrow", [1, D], F32,
                                                 side="right"))

        with ExitStack() as actx:
            cos_t = actx.enter_context(nc.sbuf_tensor("cos_t", [128, 512], F32))
            kvev = actx.enter_context(nc.sbuf_tensor("kvev", [64, D], F32))
            ksev = actx.enter_context(nc.sbuf_tensor("ksev", [1, 1024], F32))
            ssin = actx.enter_context(nc.sbuf_tensor("ssin", [128, 1024], F32))
            tgT = actx.enter_context(nc.sbuf_tensor("tgT", [128, 16], F32))
            tgI = actx.enter_context(nc.sbuf_tensor("tgI", [128, 32], F32))
            tgF = actx.enter_context(nc.sbuf_tensor("tgF", [128, 32], F32))
            ni = actx.enter_context(nc.sbuf_tensor("ni", [128, 512], I32))
            wq_sb = [actx.enter_context(nc.sbuf_tensor(f"wq{c}", [128, 3 * D], BF16))
                     for c in range(DC)]
            xln_b = [actx.enter_context(nc.sbuf_tensor(f"xln{j}", [128, D], BF16))
                     for j in range(2)]
            xlnT = [actx.enter_context(nc.sbuf_tensor(f"xlnT{j}", [128, D], BF16))
                    for j in range(2)]
            q_sb = actx.enter_context(nc.sbuf_tensor("q_sb", [128, D], F32))
            k_sb = actx.enter_context(nc.sbuf_tensor("k_sb", [128, D], F32))
            q_raw = actx.enter_context(nc.sbuf_tensor("q_raw", [128, D], F32))
            k_raw = actx.enter_context(nc.sbuf_tensor("k_raw", [128, D], F32))
            rp_sc = actx.enter_context(nc.sbuf_tensor("rp_sc", [128, D], F32))
            e_sc = actx.enter_context(nc.sbuf_tensor("e_sc", [128, D], BF16))
            r_sc = actx.enter_context(nc.sbuf_tensor("r_sc", [128, D], BF16))
            e2_sc = actx.enter_context(nc.sbuf_tensor("e2_sc", [128, D], BF16))
            r2_sc = actx.enter_context(nc.sbuf_tensor("r2_sc", [128, D], BF16))
            kf_b = actx.enter_context(nc.sbuf_tensor("kf_b", [128, D], BF16))
            v_b = actx.enter_context(nc.sbuf_tensor("v_b", [128, D], BF16))

            ps_a = actx.enter_context(nc.psum_tensor("ps_a", [128, D], F32))
            ps_b = actx.enter_context(nc.psum_tensor("ps_b", [128, D], F32))
            kv_ps = [actx.enter_context(nc.psum_tensor(f"kvps{j}", [65, 512], F32))
                     for j in range(2)]
            tp_ps = [actx.enter_context(nc.psum_tensor(f"tpps{j}", [128, 512],
                                                       BF16)) for j in range(2)]

            blockA = actx.enter_context(nc.Block("A"))

            def g_prep(g):
                g.iota(tgT[:, :], [[128, 16]], base=0, channel_multiplier=1,
                       allow_small_or_imprecise_dtypes=True).then_inc(sB.h, 1)
                g.iota(tgI[:, :], [[1, 32]], base=0, channel_multiplier=0,
                       allow_small_or_imprecise_dtypes=True).then_inc(sB.h, 1)
                g.memset(onesc[:, :], 1.0).then_inc(sB.h, 1)
                g.memset(onesr[:, :], 1.0).then_inc(sB.h, 1)
                g.memset(onesf[:, :], 1.0).then_inc(sB.h, 1)
                g.memset(zer65[:, :], 0.0).then_inc(sB.h, 1)
                g.memset(xlnT[0][:, :], 0.0).then_inc(sB.h, 1)
                g.iota(io1[:, :], [[1, 128]], base=0, channel_multiplier=0,
                       allow_small_or_imprecise_dtypes=True).then_inc(sB.h, 1)
                g.iota(io2[:, :], [[0, 128]], base=0, channel_multiplier=1,
                       allow_small_or_imprecise_dtypes=True).then_inc(sB.h, 1)
                sB.n += 8
                g.dma_start(out=tb_sb[:, :], in_=tb_d.ap()).then_inc(sG.h, 16)
                g.dma_start(out=b1T[:, :],
                            in_=ap(b1_d, 0, [[1, 128], [128, 32]]),
                            allow_slow_non_contiguous=True).then_inc(sG.h, 16)
                g.dma_start(out=b2b[:, :], in_=b2_d.ap()).then_inc(sG.h, 16)
                sG.n += 48
                assert sG.n == G_PREP
                for c in range(DC):
                    g.dma_start(out=wq_sb[c][:, :],
                                in_=wq_d.ap()[128 * c:128 * (c + 1), :]
                                ).then_inc(sW.h, 16)
                    sW.n += 16
                for c in range(DC):
                    g.dma_start(out=wo_sb[c][:, :],
                                in_=wo_d.ap()[128 * c:128 * (c + 1), :]
                                ).then_inc(sW.h, 16)
                    sW.n += 16
                for k in range(32):
                    if k >= 2:
                        g.wait_ge(sA.h, A_ABS[k - 2])
                    g.dma_start(out=binf[k % 2][:, :],
                                in_=w1n_d.ap()[128 * k:128 * (k + 1), :]
                                ).then_inc(sBF[k % 2].h, 16)
                    sBF[k % 2].n += 16
                for k in range(32):
                    r_, s_ = divmod(k, 4)
                    j = 32 + k
                    g.wait_ge(sA.h, A_ABS[j - 2])
                    g.dma_start(out=binf[j % 2][:, :],
                                in_=w2n_d.ap()[128 * r_:128 * (r_ + 1),
                                               1024 * s_:1024 * (s_ + 1)]
                                ).then_inc(sBF[j % 2].h, 16)
                    sBF[j % 2].n += 16

            blockA.gpsimd(g_prep)

            def v_prep(v):
                v.wait_ge(sB.h, B_PREP)
                v.tensor_tensor(ident[:, :], io1[:, :], io2[:, :],
                                ALU.is_equal)
                v.wait_ge(sG.h, G_PREP)
                v.tensor_scalar_add(tgT[:, :], tgT[:, :], tb_sb[:, :]
                                    ).then_inc(sV.h, 1)
                sV.n += 1
                v.wait_ge(sV.h, 1)
                v.wait_ge(sA.h, A_INVF)
                last = None
                for j in range(16):
                    last = v.tensor_scalar_mul(q_raw[:, 32 * j:32 * (j + 1)],
                                               tgF[:, :], tgT[:, j:j + 1])
                vch(v, last)
                vch(v, v.tensor_scalar_mul(k_raw[:, 0:512], q_raw[:, 0:512], 1.0 / TWO_PI))
                vch(v, v.tensor_copy(ni[:, :], k_raw[:, 0:512]))
                vch(v, v.tensor_copy(k_raw[:, 0:512], ni[:, :]))
                vch(v, v.scalar_tensor_tensor(q_raw[:, 512:1024], k_raw[:, 0:512], -CW1,
                                              q_raw[:, 0:512], ALU.mult, ALU.add))
                vch(v, v.scalar_tensor_tensor(q_raw[:, 512:1024], k_raw[:, 0:512], -CW2,
                                              q_raw[:, 512:1024], ALU.mult, ALU.add))
                vch(v, v.scalar_tensor_tensor(q_raw[:, 512:1024], k_raw[:, 0:512], -CW3,
                                              q_raw[:, 512:1024], ALU.mult, ALU.add))
                vch(v, v.tensor_single_scalar(z_sc[:, 0:512], q_raw[:, 512:1024],
                                              float(np.pi), ALU.is_gt))
                vch(v, v.scalar_tensor_tensor(q_raw[:, 512:1024], z_sc[:, 0:512], -CW1,
                                              q_raw[:, 512:1024], ALU.mult, ALU.add))
                vch(v, v.scalar_tensor_tensor(q_raw[:, 512:1024], z_sc[:, 0:512], -CW2,
                                              q_raw[:, 512:1024], ALU.mult, ALU.add))
                v.scalar_tensor_tensor(q_raw[:, 512:1024], z_sc[:, 0:512], -CW3,
                                       q_raw[:, 512:1024], ALU.mult, ALU.add
                                       ).then_inc(sV.h, 1)
                sV.n += 1
                v.wait_ge(sV.h, sV.n)
                v.wait_ge(sA.h, A_SIN)
                vch(v, v.tensor_scalar_add(k_raw[:, 0:512], q_raw[:, 512:1024],
                                           float(np.pi / 2)))
                vch(v, v.tensor_single_scalar(z_sc[:, 0:512], k_raw[:, 0:512],
                                              float(np.pi), ALU.is_gt))
                vch(v, v.scalar_tensor_tensor(k_raw[:, 0:512], z_sc[:, 0:512], -CW1,
                                              k_raw[:, 0:512], ALU.mult, ALU.add))
                vch(v, v.scalar_tensor_tensor(k_raw[:, 0:512], z_sc[:, 0:512], -CW2,
                                              k_raw[:, 0:512], ALU.mult, ALU.add))
                v.scalar_tensor_tensor(k_raw[:, 0:512], z_sc[:, 0:512], -CW3,
                                       k_raw[:, 0:512], ALU.mult, ALU.add
                                       ).then_inc(sV.h, 1)
                sV.n += 1
                v.wait_ge(sV.h, sV.n)
                v.tensor_scalar_mul(
                    ap(ssin, 0, [[1024, 128], [64, 16], [1, 32]]),
                    ap(q_raw, 0, [[1024, 128], [32, 16], [1, 32]]), -1.0)
                v.tensor_copy(
                    ap(ssin, 32, [[1024, 128], [64, 16], [1, 32]]),
                    ap(q_raw, 0, [[1024, 128], [32, 16], [1, 32]])).then_inc(sV.h, 1)
                sV.n += 1
                assert sV.n == V_PREP

            blockA.vector(v_prep)

            def s_prep(s):
                s.wait_ge(sB.h, 2)
                s.activation(tgF[:, :], tgI[:, :], AF.Exp,
                             scale=-LN1E4 / 32.0).then_inc(sA.h, 1)
                sA.n += 1
                s.wait_ge(sV.h, 2)
                s.activation(q_raw[:, 0:512], q_raw[:, 512:1024], AF.Sin).then_inc(sA.h, 1)
                sA.n += 1
                s.wait_ge(sV.h, 3)
                s.activation(cos_t[:, :], k_raw[:, 0:512], AF.Sin).then_inc(sA.h, 1)
                sA.n += 1
                assert sA.n == A_COS

            blockA.scalar(s_prep)

            def sy_A(sy):
                for i in range(4):
                    sy.dma_start(out=xf[i][:, :],
                                 in_=x_d.ap()[128 * i:128 * (i + 1), :]
                                 ).then_inc(sXF[i].h, 16)
                    sXF[i].n += 16
                for i in range(NT):
                    if i + 4 < NT:
                        sy.wait_ge(sV.h, v_xln(i))
                        sy.wait_ge(sA.h, A_SSQ[i])
                        sy.dma_start(out=xf[(i + 4) % 4][:, :],
                                     in_=x_d.ap()[128 * (i + 4):128 * (i + 5), :]
                                     ).then_inc(sXF[(i + 4) % 4].h, 16)
                        sXF[(i + 4) % 4].n += 16

            blockA.sync(sy_A)

            def v_A(v):
                d4 = [[1024, 128], [64, 16], [32, 2], [1, 32]]

                def ln_stage(i):
                    xt = xf[i % 4]
                    v.wait_ge(sXF[i % 4].h, XFA(i))
                    vch(v, v.reduce_sum(st_sum[:, :], xt[:, :], axis=AX.X))
                    vch(v, v.tensor_scalar_mul(st_nm[:, :], st_sum[:, :],
                                               -1.0 / D))
                    v.wait_ge(sA.h, A_SSQ[i])
                    v.tensor_scalar_mul(st_v1[:, :], st_ssq[:, :], 1.0 / D)
                    vch(v, v.tensor_mul(st_nm2[:, :], st_nm[:, :], st_nm[:, :]))
                    v.scalar_tensor_tensor(st_var[:, :], st_v1[:, :], 1e-5,
                                           st_nm2[:, :], ALU.add, ALU.subtract
                                           ).then_inc(sV.h, 1)
                    sV.n += 1
                    assert sV.n == v_var(i)
                    v.wait_ge(sA.h, A_STD[i])
                    v.reciprocal(st_rstd[:, :], st_std[:, :]).then_inc(sV.h, 1)
                    sV.n += 1
                    v.wait_ge(sV.h, v_rstd(i))
                    if i >= 2:
                        v.wait_ge(sM.h, M_TPA[(i - 2, DC - 1)])
                    vch(v, v.tensor_scalar_add(z_sc[:, :], xt[:, :],
                                               st_nm[:, :]))
                    v.tensor_scalar_mul(xln_b[i % 2][:, :], z_sc[:, :],
                                        st_rstd[:, :]).then_inc(sV.h, 1)
                    sV.n += 1
                    assert sV.n == v_xln(i)

                def attn_stage(j):
                    cos_ap = ap(cos_t, 32 * j,
                                [[512, 128], [0, 16], [0, 2], [1, 32]])
                    sin_ap = ap(ssin, 64 * j,
                                [[1024, 128], [0, 16], [32, 2], [1, 32]])
                    if j == 0:
                        v.wait_ge(sA.h, A_COS)
                    v.wait_ge(sA.h, A_EVQ[j])
                    if j >= 1:
                        v.wait_ge(sA.h, A_EQ[j - 1])
                    v.tensor_tensor(ap(q_sb, 0, d4), ap(q_raw, 0, d4),
                                    cos_ap, ALU.mult)
                    swap = ap(q_raw, 32,
                              [[1024, 128], [64, 16], [-32, 2], [1, 32]])
                    vch(v, v.tensor_tensor(ap(rp_sc, 0, d4), swap, sin_ap,
                                           ALU.mult))
                    v.tensor_add(q_sb[:, :], q_sb[:, :], rp_sc[:, :]
                                 ).then_inc(sV.h, 1)
                    sV.n += 1
                    assert sV.n == v_ropeq(j)
                    v.wait_ge(sA.h, A_EVK[j])
                    if j >= 1:
                        v.wait_ge(sA.h, A_EK[j - 1])
                    v.tensor_tensor(ap(k_sb, 0, d4), ap(k_raw, 0, d4),
                                    cos_ap, ALU.mult)
                    swapk = ap(k_raw, 32,
                               [[1024, 128], [64, 16], [-32, 2], [1, 32]])
                    vch(v, v.tensor_tensor(ap(rp_sc, 0, d4), swapk, sin_ap,
                                           ALU.mult))
                    v.tensor_add(k_sb[:, :], k_sb[:, :], rp_sc[:, :]
                                 ).then_inc(sV.h, 1)
                    sV.n += 1
                    assert sV.n == v_ropek(j)
                    v.wait_ge(sA.h, A_EQ[j])
                    v.scalar_tensor_tensor(qf_all[:, D * j:D * (j + 1)],
                                           e_sc[:, :], 1.0, r_sc[:, :],
                                           ALU.min, ALU.add).then_inc(sV.h, 1)
                    sV.n += 1
                    assert sV.n == v_qf(j)
                    v.wait_ge(sA.h, A_EK[j])
                    if j >= 1:
                        v.wait_ge(sM.h, M_KVd[j - 1])
                    v.scalar_tensor_tensor(kf_b[:, :], e2_sc[:, :], 1.0,
                                           r2_sc[:, :], ALU.min, ALU.add
                                           ).then_inc(sV.h, 1)
                    sV.n += 1
                    assert sV.n == v_kf(j)

                for i in range(NT + 2):
                    if i == 0:
                        ln_stage(0)
                        ln_stage(1)
                    if i >= 2:
                        attn_stage(i - 2)
                    if i < NT:
                        v.wait_ge(sM.h, M_TPA[(i, DC - 1)])
                        for bk in range(2):
                            v.tensor_copy(
                                ap(xlnT[i % 2], 128 * bk,
                                   [[1024, 128], [256, 4], [1, 128]]),
                                ap(tp_ps[bk], 0,
                                   [[512, 128], [128, 4], [1, 128]])
                                ).then_inc(sT2[i % 2].h, 1)
                            sT2[i % 2].n += 1
                    if i + 2 < NT:
                        ln_stage(i + 2)
                v.wait_ge(sM.h, M_END_A - 1)
                v.tensor_copy(kvev[:, 0:512], kv_ps[0][0:64, :])
                v.tensor_copy(kvev[:, 512:1024], kv_ps[1][0:64, :])
                v.tensor_copy(ksev[:, 0:512], kv_ps[0][64:65, :])
                v.tensor_copy(ksev[:, 512:1024], kv_ps[1][64:65, :]
                              ).then_inc(sV.h, 1)
                sV.n += 1
                assert sV.n == V_KVEV
                v.wait_ge(sM.h, M_END_A)
                v.tensor_copy(ksb[:, :], ps_a[:, :]).then_inc(sV.h, 1)
                sV.n += 1
                assert sV.n == V_KSB

            blockA.vector(v_A)

            def s_A(s):
                def ssq_std(k):
                    xt = xf[k % 4]
                    s.wait_ge(sXF[k % 4].h, XFA(k))
                    if k >= 1:
                        s.wait_ge(sV.h, v_var(k - 1))
                    s.activation(junk[:, :], xt[:, :], AF.Square,
                                 accum_out=st_ssq[:, :]).then_inc(sA.h, 1)
                    sA.n += 1
                    assert sA.n == A_SSQ[k]
                    s.wait_ge(sV.h, v_var(k))
                    s.activation(st_std[:, :], st_var[:, :], AF.Sqrt
                                 ).then_inc(sA.h, 1)
                    sA.n += 1
                    assert sA.n == A_STD[k]

                absj_ = 0
                for i in range(NT + 2):
                    if i == 0:
                        for k in range(3):
                            ssq_std(k)
                    if i >= 2:
                        j = i - 2
                        psq = ps_a if j % 2 == 0 else ps_b
                        s.wait_ge(sV.h, v_ropeq(j))
                        if j >= 1:
                            s.wait_ge(sV.h, v_qf(j - 1))
                        s.activation(e_sc[:, :], q_sb[:, :], AF.Exp,
                                     scale=QSCALE)
                        s.activation(r_sc[:, :], q_sb[:, :], AF.Relu,
                                     scale=QSCALE).then_inc(sA.h, 1)
                        sA.n += 1
                        assert sA.n == A_EQ[j]
                        s.wait_ge(sV.h, v_ropek(j))
                        if j >= 1:
                            s.wait_ge(sV.h, v_kf(j - 1))
                        s.activation(e2_sc[:, :], k_sb[:, :], AF.Exp)
                        s.activation(r2_sc[:, :], k_sb[:, :], AF.Relu
                                     ).then_inc(sA.h, 1)
                        sA.n += 1
                        assert sA.n == A_EK[j]
                        s.wait_ge(sM.h, M_Vd[j])
                        s.activation(v_b[:, :], psq[:, :], AF.Copy
                                     ).then_inc(sA.h, 1)
                        sA.n += 1
                        assert sA.n == A_VB[j]
                    if 1 <= i <= NT:
                        j = i - 1
                        psq = ps_a if j % 2 == 0 else ps_b
                        psk = ps_b if j % 2 == 0 else ps_a
                        s.wait_ge(sM.h, M_Qd[j])
                        if j >= 1:
                            s.wait_ge(sV.h, v_ropeq(j - 1))
                        s.activation(q_raw[:, :], psq[:, :], AF.Copy
                                     ).then_inc(sA.h, 1)
                        sA.n += 1
                        assert sA.n == A_EVQ[j]
                        s.wait_ge(sM.h, M_Kd[j])
                        if j >= 1:
                            s.wait_ge(sV.h, v_ropek(j - 1))
                        s.activation(k_raw[:, :], psk[:, :], AF.Copy
                                     ).then_inc(sA.h, 1)
                        sA.n += 1
                        assert sA.n == A_EVK[j]
                    if i >= 1 and i + 2 < NT:
                        ssq_std(i + 2)
                    if i >= 2:
                        for _ in range(ABS_SCHED[i - 2]):
                            jj = absj_
                            s.wait_ge(sBF[jj % 2].h, BF(jj))
                            dst = (a1n[:, jj:jj + 1] if jj < 32
                                   else a2n4[:, jj - 32:jj - 31])
                            s.activation(junk[:, :], binf[jj % 2][:, :],
                                         AF.Abs, accum_out=dst
                                         ).then_inc(sA.h, 1)
                            sA.n += 1
                            absj_ += 1
                assert sA.n == A_END_A

            blockA.scalar(s_A)

            def t_A(t):
                t.wait_ge(sB.h, B_PREP)
                for n in range(2):
                    t.matmul(kv_ps[n][0:65, :], zer65[:, :],
                             xlnT[0][:, 0:512], start=True, stop=True,
                             skip_group_check=True)

                def kv_mms(j):
                    t.wait_ge(sV.h, v_kf(j))
                    t.wait_ge(sA.h, A_VB[j])
                    last = None
                    for h in range(NH):
                        last = t.matmul(
                            kv_ps[h // 8][0:64, 64 * (h % 8):64 * (h % 8) + 64],
                            kf_b[:, 64 * h:64 * (h + 1)],
                            v_b[:, 64 * h:64 * (h + 1)],
                            start=False, stop=(j == NT - 1),
                            skip_group_check=True)
                    last.then_inc(sM.h, 1)
                    sM.n += 1
                    assert sM.n == M_KVd[j]
                    for n in range(2):
                        last = t.matmul(kv_ps[n][64:65, 0:512],
                                        onesc[:, :],
                                        kf_b[:, 512 * n:512 * (n + 1)],
                                        start=False, stop=(j == NT - 1),
                                        tile_position=(0, 64),
                                        skip_group_check=True)
                    last.then_inc(sM.h, 1)
                    sM.n += 1
                    assert sM.n == M_KSd[j]

                for i in range(NT):
                    psq = ps_a if i % 2 == 0 else ps_b
                    psk = ps_b if i % 2 == 0 else ps_a
                    xT = xlnT[i % 2]
                    t.wait_ge(sV.h, v_xln(i))
                    if i >= 1:
                        t.wait_ge(sT2[(i - 1) % 2].h, STx(i - 1))
                    for c in range(DC):
                        t.transpose(tp_ps[c % 2][:, 128 * (c // 2):
                                                 128 * (c // 2) + 128],
                                    xln_b[i % 2][:, 128 * c:128 * (c + 1)],
                                    ident[:, :]).then_inc(sM.h, 1)
                        sM.n += 1
                        assert sM.n == M_TPA[(i, c)]
                    t.wait_ge(sT2[i % 2].h, STx(i))
                    if i == 0:
                        t.wait_ge(sW.h, 256)
                    if i >= 1:
                        t.wait_ge(sA.h, A_EVK[i - 1])
                    last = None
                    for c in range(DC):
                        for n in range(2):
                            last = t.matmul(psq[:, 512 * n:512 * (n + 1)],
                                            xT[:, 128 * c:128 * (c + 1)],
                                            wq_sb[c][:, 512 * n:512 * (n + 1)],
                                            start=(c == 0), stop=(c == DC - 1),
                                            skip_group_check=True)
                    last.then_inc(sM.h, 1)
                    sM.n += 1
                    assert sM.n == M_Qd[i]
                    if i >= 1:
                        t.wait_ge(sA.h, A_VB[i - 1])
                    for c in range(DC):
                        for n in range(2):
                            last = t.matmul(
                                psk[:, 512 * n:512 * (n + 1)],
                                xT[:, 128 * c:128 * (c + 1)],
                                wq_sb[c][:, 1024 + 512 * n:1024 + 512 * (n + 1)],
                                start=(c == 0), stop=(c == DC - 1),
                                skip_group_check=True)
                    last.then_inc(sM.h, 1)
                    sM.n += 1
                    assert sM.n == M_Kd[i]
                    if i >= 1:
                        kv_mms(i - 1)
                    t.wait_ge(sA.h, A_EVQ[i])
                    for c in range(DC):
                        for n in range(2):
                            last = t.matmul(
                                psq[:, 512 * n:512 * (n + 1)],
                                xT[:, 128 * c:128 * (c + 1)],
                                wq_sb[c][:, 2048 + 512 * n:2048 + 512 * (n + 1)],
                                start=(c == 0), stop=(c == DC - 1),
                                skip_group_check=True)
                    last.then_inc(sM.h, 1)
                    sM.n += 1
                    assert sM.n == M_Vd[i]
                kv_mms(NT - 1)
                t.wait_ge(sG.h, G_KSR)
                t.wait_ge(sA.h, A_EVK[NT - 1])
                last = None
                for n in range(2):
                    last = t.matmul(ps_a[:, 512 * n:512 * (n + 1)],
                                    onesf[0:1, :],
                                    ksrow[0:1, 512 * n:512 * (n + 1)],
                                    start=True, stop=True,
                                    skip_group_check=True)
                last.then_inc(sM.h, 1)
                sM.n += 1
                assert sM.n == M_END_A

            blockA.tensor(t_A)

            def g_cc(g):
                g.wait_ge(sV.h, V_KVEV)
                g.dma_start(out=ap(cci_d, 0, [[1024, 64], [1, 1024]]),
                            in_=kvev[:, :]).then_inc(sG.h, 16)
                g.dma_start(out=cci_d.ap()[64:65, :],
                            in_=ksev[:, :]).then_inc(sG.h, 16)
                sG.n += 32
                assert sG.n == G_KVB
                g.wait_ge(sG.h, G_KVB)
                g.collective_compute(
                    "AllReduce", ALU.add,
                    replica_groups=[[0, 1, 2, 3], [4, 5, 6, 7]],
                    ins=[cci_d.ap().opt()],
                    outs=[cco_d.ap().opt()],
                ).then_inc(sC.h, 1)
                sC.n += 1
                g.wait_ge(sC.h, 1)
                g.dma_start(out=kv2[0:64, :].rearrange("p (c v) -> p c v", v=64),
                            in_=ap(cco_d, 0, [[1024, 64], [128, 8], [1, 64]])
                            ).then_inc(sG.h, 16)
                g.dma_start(out=kv2[64:128, :].rearrange("p (c v) -> p c v", v=64),
                            in_=ap(cco_d, 64, [[1024, 64], [128, 8], [1, 64]])
                            ).then_inc(sG.h, 16)
                sG.n += 32
                assert sG.n == G_KV2
                g.dma_start(out=ksrow[:, :],
                            in_=cco_d.ap()[64:65, :]).then_inc(sG.h, 16)
                sG.n += 16
                assert sG.n == G_KSR
                if dump:
                    g.dma_start(out=dmp_cc.ap(), in_=cco_d.ap()).then_inc(sD.h, 16)
                    g.dma_start(out=dmp_qf.ap(), in_=qf_all[:, :]
                                ).then_inc(sD.h, 16)
                    g.dma_start(out=dmp_xln.ap(), in_=dx_sb[:, :]
                                ).then_inc(sD.h, 16)
                    g.dma_start(out=dmp_q.ap(), in_=dq_sb[:, :]
                                ).then_inc(sD.h, 16)
                    g.dma_start(out=dmp_st.ap(), in_=ds_sb[:, :]
                                ).then_inc(sD.h, 16)
                    g.wait_ge(sD.h, 80)

            blockA.gpsimd(g_cc)

        if phases < 2:
            return nc
        bw1 = [root.enter_context(nc.sbuf_tensor(f"bw1_{c}", [128, FH], BF16))
               for c in range(DC)]
        binf2 = [root.enter_context(nc.sbuf_tensor(f"binf2_{j}", [128, D], F32))
                 for j in range(2)]
        blockB = root.enter_context(nc.Block("B"))

        with ExitStack() as b1ctx:
            arep1 = b1ctx.enter_context(nc.sbuf_tensor("arep1", [128, FH],
                                                        BF16))
            qfd = [b1ctx.enter_context(nc.sbuf_tensor(f"qfd{j}", [128, D], BF16))
                   for j in range(2)]
            z_sc2 = b1ctx.enter_context(nc.sbuf_tensor("z_sc2", [128, D], F32))
            den_all = b1ctx.enter_context(nc.sbuf_tensor("den_all",
                                                         [128, NT * NH], F32))
            rden_all = b1ctx.enter_context(nc.sbuf_tensor("rden_all",
                                                          [128, NT * NH], F32))
            qfdT = b1ctx.enter_context(nc.sbuf_tensor("qfdT", [128, DC * GT], BF16))
            attT = b1ctx.enter_context(nc.sbuf_tensor("attT", [128, DC * GT], BF16))
            x1sb = [b1ctx.enter_context(nc.sbuf_tensor(f"x1sb{j}", [128, D], F32))
                    for j in range(2)]
            att_ps = [b1ctx.enter_context(nc.psum_tensor(f"attps{j}", [128, 512],
                                                         F32)) for j in range(4)]
            op_ps = b1ctx.enter_context(nc.psum_tensor("opps", [128, D], F32))
            tp2 = [b1ctx.enter_context(nc.psum_tensor(f"tp2_{j}", [128, 512],
                                                      BF16)) for j in range(2)]

            def g_B1(g):
                for k in range(32):
                    qq, dc = divmod(k, 8)
                    if k >= 2:
                        g.wait_ge(sV.h, VL_BIN1[k - 2])
                    g.dma_start(out=binf2[k % 2][:, :],
                                in_=w1t_d.ap()[128 * dc:128 * (dc + 1),
                                               1024 * qq:1024 * (qq + 1)]
                                ).then_inc(sBG[k % 2].h, 16)
                    sBG[k % 2].n += 16
                g.wait_ge(sV.h, V_A1F)
                g.dma_start(out=ap(a2_d, 0, [[1, 128], [128, 8]]),
                            in_=a2n[:, :],
                            allow_slow_non_contiguous=True).then_inc(sG.h, 16)
                sG.n += 16
                assert sG.n == G_A2B
                g.wait_ge(sG.h, G_A2B)
                g.dma_start(out=a2row[:, :], in_=a2_d.ap()).then_inc(sG.h, 16)
                sG.n += 16
                assert sG.n == G_A2R

            blockB.gpsimd(g_B1)

            def sy_B1(sy):
                for i in range(4):
                    sy.wait_ge(sV.h, v_xln(LAST_A_XF[i]))
                    sy.wait_ge(sA.h, A_SSQ[LAST_A_XF[i]])
                    sy.dma_start(out=xf[i][:, :],
                                 in_=x_d.ap()[128 * i:128 * (i + 1), :]
                                 ).then_inc(sXF[i].h, 16)
                    sXF[i].n += 16
                for i in range(NT):
                    if i >= 1:
                        sy.wait_ge(sV.h, v_x1(i - 1))
                        sy.dma_start(out=x1_d.ap()[128 * (i - 1):128 * i, :],
                                     in_=x1sb[(i - 1) % 2][:, :]
                                     ).then_inc(sO1[(i - 1) % 2].h, 16)
                        sO1[(i - 1) % 2].n += 16
                        if i + 3 < NT:
                            sy.dma_start(
                                out=xf[(i + 3) % 4][:, :],
                                in_=x_d.ap()[128 * (i + 3):128 * (i + 4), :]
                                ).then_inc(sXF[(i + 3) % 4].h, 16)
                            sXF[(i + 3) % 4].n += 16
                sy.wait_ge(sV.h, v_x1(NT - 1))
                sy.dma_start(out=x1_d.ap()[128 * (NT - 1):128 * NT, :],
                             in_=x1sb[(NT - 1) % 2][:, :]
                             ).then_inc(sO1[(NT - 1) % 2].h, 16)
                sO1[(NT - 1) % 2].n += 16

            blockB.sync(sy_B1)

            def v_B1(v):
                zbuf = [z_sc, z_sc2]
                zv = {}
                for i in range(NT + 1):
                    if i < NT:
                        if i >= 2:
                            v.wait_ge(sVC.h, zv[i - 2] + 1)
                        v.tensor_tensor(zbuf[i % 2][:, :],
                                        qf_all[:, D * i:D * (i + 1)],
                                        ksb[:, :], ALU.mult
                                        ).then_inc(sVC.h, 1)
                        sVC.n += 1
                        zv[i] = sVC.n
                    if i >= 1:
                        j = i - 1
                        v.wait_ge(sVC.h, zv[j])
                        v.reduce_sum(
                            den_all[:, NH * j:NH * (j + 1)],
                            zbuf[j % 2][:].rearrange("p (h d) -> p h d", d=HD),
                            axis=AX.X).then_inc(sVC.h, 1)
                        sVC.n += 1
                v.wait_ge(sVC.h, sVC.n)
                vch(v, v.tensor_scalar_max(den_all[:, :], den_all[:, :], 1e-6))
                v.reciprocal(rden_all[:, :], den_all[:, :]).then_inc(sV.h, 1)
                sV.n += 1
                assert sV.n == V_DEN
                def clamp1(k):
                    qq, dc = divmod(k, 8)
                    v.wait_ge(sBG[k % 2].h, BG(k))
                    sl = bw1[dc][:, 1024 * qq:1024 * (qq + 1)]
                    vch(v, v.tensor_scalar(z_sc[:, :], binf2[k % 2][:, :],
                                           1e38, 1.0, ALU.mult, ALU.min))
                    v.tensor_scalar_max(sl, z_sc[:, :], -1.0
                                        ).then_inc(sV.h, 1)
                    sV.n += 1
                    assert sV.n == VL_BIN1[k]

                def qfdT_evac(i):
                    tt = i % GNT
                    v.wait_ge(sM.h, m_tp1(i, DC - 1))
                    if i // GNT >= 1:
                        v.wait_ge(sM.h, m_att(i // GNT - 1, 7))
                    for bk in range(2):
                        v.tensor_copy(
                            ap(qfdT, 256 * bk + 128 * tt,
                               [[2048, 128], [512, 4], [1, 128]]),
                            ap(tp2[bk], 0, [[512, 128], [128, 4], [1, 128]])
                            ).then_inc(sTB2[i % 2].h, 1)
                        sTB2[i % 2].n += 1

                for g in range(NG):
                    for tt in range(GNT):
                        i = GNT * g + tt
                        if i >= 2:
                            v.wait_ge(sM.h, m_tp1(i - 2, DC - 1))
                        v.tensor_tensor(
                            qfd[i % 2][:].rearrange("p (h d) -> p h d", d=HD),
                            qf_all[:, D * i:D * (i + 1)].rearrange(
                                "p (h d) -> p h d", d=HD),
                            ap(rden_all, NH * i, [[NT * NH, 128], [1, 16],
                                                  [0, 64]]),
                            ALU.mult).then_inc(sV.h, 1)
                        sV.n += 1
                        assert sV.n == v_qfd(i)
                    clamp1(4 * g)
                    qfdT_evac(GNT * g)
                    clamp1(4 * g + 1)
                    qfdT_evac(GNT * g + 1)
                    for tt in range(GNT):
                        i = GNT * g + tt
                        v.wait_ge(sM.h, m_op(g, tt))
                        v.wait_ge(sXF[i % 4].h, XFB(i))
                        if i >= 2:
                            v.wait_ge(sO1[i % 2].h, SO1(i - 2))
                        v.tensor_add(x1sb[i % 2][:, :], op_ps[:, :],
                                     xf[i % 4][:, :]).then_inc(sV.h, 1)
                        sV.n += 1
                        assert sV.n == v_x1(i)
                    clamp1(4 * g + 2)
                    clamp1(4 * g + 3)
                v.wait_ge(sA.h, A_ABS[63])
                v.tensor_scalar_mul(a1n[:, :], a1n[:, :], 1.0 / D)
                vch(v, v.reduce_sum(a2n[:, :],
                                    a2n4[:].rearrange("p (r s) -> p r s", s=4),
                                    axis=AX.X))
                v.tensor_scalar_mul(a2n[:, :], a2n[:, :], 1.0 / FH
                                    ).then_inc(sV.h, 1)
                sV.n += 1
                assert sV.n == V_A1F

            blockB.vector(v_B1)

            def s_B1(s):
                for g in range(NG):
                    s.wait_ge(sM.h, m_att(g, DC - 1))
                    for c in range(DC):
                        s.activation(attT[:, GT * c:GT * (c + 1)],
                                     att_ps[c // 2][:, 256 * (c % 2):
                                                    256 * (c % 2) + 256],
                                     AF.Copy).then_inc(sA.h, 1)
                        sA.n += 1
                assert sA.n == A_END_B1

            blockB.scalar(s_B1)

            def t_B1(t):
                t.wait_ge(sG.h, G_KV2)
                for g in range(NG):
                    for tt in range(GNT):
                        i = GNT * g + tt
                        t.wait_ge(sV.h, v_qfd(i))
                        if i >= 1:
                            t.wait_ge(sTB2[(i - 1) % 2].h, TBQ(i - 1))
                        for c in range(DC):
                            t.transpose(tp2[c % 2][:, 128 * (c // 2):
                                                   128 * (c // 2) + 128],
                                        qfd[i % 2][:, 128 * c:128 * (c + 1)],
                                        ident[:, :]).then_inc(sM.h, 1)
                            sM.n += 1
                            assert sM.n == m_tp1(i, c)
                    t.wait_ge(sTB2[0].h, TBQ(GNT * g))
                    t.wait_ge(sTB2[1].h, TBQ(GNT * g + 1))
                    if g >= 1:
                        t.wait_ge(sA.h, a_att(g - 1, DC - 1))
                    for c in range(DC):
                        t.matmul(att_ps[c // 2][0:64, 256 * (c % 2):
                                                256 * (c % 2) + 256],
                                 kv2[0:64, 64 * c:64 * (c + 1)],
                                 qfdT[0:64, GT * c:GT * (c + 1)],
                                 start=True, stop=True, tile_position=(0, 0),
                                 skip_group_check=True)
                        last = t.matmul(att_ps[c // 2][64:128, 256 * (c % 2):
                                                       256 * (c % 2) + 256],
                                        kv2[64:128, 64 * c:64 * (c + 1)],
                                        qfdT[64:128, GT * c:GT * (c + 1)],
                                        start=True, stop=True,
                                        tile_position=(64, 64),
                                        skip_group_check=True)
                        last.then_inc(sM.h, 1)
                        sM.n += 1
                        assert sM.n == m_att(g, c)
                    if g == 0:
                        t.wait_ge(sW.h, 256)
                    t.wait_ge(sA.h, a_att(g, DC - 1))
                    for tt in range(GNT):
                        i = GNT * g + tt
                        if i >= 1:
                            t.wait_ge(sV.h, v_x1(i - 1))
                        last = None
                        for c in range(DC):
                            for n in range(2):
                                last = t.matmul(
                                    op_ps[:, 512 * n:512 * (n + 1)],
                                    attT[:, GT * c + 128 * tt:
                                         GT * c + 128 * (tt + 1)],
                                    wo_sb[c][:, 512 * n:512 * (n + 1)],
                                    start=(c == 0), stop=(c == DC - 1),
                                    skip_group_check=True)
                        last.then_inc(sM.h, 1)
                        sM.n += 1
                        assert sM.n == m_op(g, tt)
                assert sM.n == M_END_B1

            blockB.tensor(t_B1)

            if debug:
                def g_dbg(g):
                    g.wait_ge(sV.h, v_xln(0))
                    g.dma_start(out=dbg_xln.ap(), in_=xln_b[0][:, :]
                                ).then_inc(sD.h, 16)
                    g.wait_ge(sV.h, v_qf(0))
                    g.dma_start(out=dbg_qf.ap(), in_=qf_all[:, 0:D]
                                ).then_inc(sD.h, 16)
                    g.wait_ge(sV.h, v_kf(0))
                    g.dma_start(out=dbg_kf.ap(), in_=kf_b[:, :]).then_inc(sD.h, 16)
                    g.wait_ge(sC.h, 1)
                    g.dma_start(out=dbg_cc.ap(), in_=cco_d.ap()).then_inc(sD.h, 16)
                    g.wait_ge(sV.h, v_qfd(0))
                    g.dma_start(out=dbg_den.ap(), in_=den[:, :]).then_inc(sD.h, 16)
                    g.wait_ge(sA.h, a_att(0, DC - 1))
                    g.dma_start(out=dbg_att.ap(), in_=attT[:, :]).then_inc(sD.h, 16)
                    g.wait_ge(sV.h, v_x1(0))
                    g.dma_start(out=dbg_x1.ap(), in_=x1sb[0][:, :]
                                ).then_inc(sD.h, 16)
                    g.wait_ge(sD.h, 112)
                blockB.gpsimd(g_dbg)

        if phases < 3:
            return nc
        ab1.close()
        w2b = [root.enter_context(nc.sbuf_tensor(f"w2b_{c}", [128, D], BF16,
                                                 side="right"))
               for c in range(NHC)]
        xl2T = [root.enter_context(nc.sbuf_tensor(f"xl2T{j}", [128, DC * GT],
                                                  BF16)) for j in range(2)]
        hTb = root.enter_context(nc.sbuf_tensor("hTb", [128, NHC * GT], BF16))
        outsb = [root.enter_context(nc.sbuf_tensor(f"outsb{j}", [128, D], F32,
                                                   side="right"))
                 for j in range(2)]
        xln_b2 = [root.enter_context(nc.sbuf_tensor(f"xln2_{j}", [128, D], BF16,
                                                    side="right"))
                  for j in range(2)]
        h_ps = [root.enter_context(nc.psum_tensor(f"hps{j}", [128, GT], F32))
                for j in range(2)]
        o2_ps = [root.enter_context(nc.psum_tensor(f"o2ps{j}", [128, D], F32))
                 for j in range(2)]

        def g_B2(g):
            for k in range(NHC):
                if k < 2:
                    g.wait_ge(sV.h, VL_BIN1[30 + k])
                else:
                    g.wait_ge(sV.h, VL_BIN2[k - 2])
                g.dma_start(out=binf2[k % 2][:, :],
                            in_=w2t_d.ap()[128 * k:128 * (k + 1), :]
                            ).then_inc(sBG[k % 2].h, 16)
                sBG[k % 2].n += 16

        blockB.gpsimd(g_B2)

        def s_B2(s):
            s.wait_ge(sV.h, V_A1F)
            for g in range(NG):
                for tt in range(GNT):
                    i = GNT * g + tt
                    s.wait_ge(sXF[i % 4].h, XFG(i))
                    if i >= 1:
                        s.wait_ge(sV.h, VL_V2VAR[i - 1])
                    s.activation(junk[:, :], xf[i % 4][:, :], AF.Square,
                                 accum_out=st_ssq[:, :]).then_inc(sA.h, 1)
                    sA.n += 1
                    s.wait_ge(sV.h, VL_V2VAR[i])
                    s.activation(st_std[:, :], st_var[:, :], AF.Sqrt
                                 ).then_inc(sA.h, 1)
                    sA.n += 1
                for tt in range(GNT):
                    i = GNT * g + tt
                    s.wait_ge(sV.h, VL_V2XLN[i])
                    if g >= 2 and tt == 0:
                        s.wait_ge(sM.h, m_ffn1(g - 2, NHC - 1))
                    for c in range(DC):
                        s.dma_start(
                            out=xl2T[g % 2][:, GT * c + 128 * tt:
                                            GT * c + 128 * (tt + 1)],
                            in_=xln_b2[i % 2][:, 128 * c:128 * (c + 1)],
                            transpose=True).then_inc(sTB2[i % 2].h, 16)
                    sTB2[i % 2].n += 128
                for hc in range(NHC):
                    s.wait_ge(sM.h, m_ffn1(g, hc))
                    s.activation(hTb[:, GT * hc:GT * (hc + 1)],
                                 h_ps[hc % 2][:, :], AF.Gelu,
                                 bias=b1T[:, hc:hc + 1],
                                 scale=a1n[:, hc:hc + 1]).then_inc(sA.h, 1)
                    sA.n += 1
            assert sA.n == A_FINAL

        blockB.scalar(s_B2)

        def v_B2(v):
            def ln2(i):
                xt = xf[i % 4]
                v.wait_ge(sXF[i % 4].h, XFG(i))
                vch(v, v.reduce_sum(st_sum[:, :], xt[:, :], axis=AX.X))
                vch(v, v.tensor_scalar_mul(st_nm[:, :], st_sum[:, :], -1.0 / D))
                v.wait_ge(sA.h, SA2_SSQ[i])
                v.tensor_scalar_mul(st_v1[:, :], st_ssq[:, :], 1.0 / D)
                vch(v, v.tensor_mul(st_nm2[:, :], st_nm[:, :], st_nm[:, :]))
                v.scalar_tensor_tensor(st_var[:, :], st_v1[:, :], 1e-5,
                                       st_nm2[:, :], ALU.add, ALU.subtract
                                       ).then_inc(sV.h, 1)
                sV.n += 1
                assert sV.n == VL_V2VAR[i]
                v.wait_ge(sA.h, SA2_STD[i])
                v.reciprocal(st_rstd[:, :], st_std[:, :]).then_inc(sV.h, 1)
                sV.n += 1
                v.wait_ge(sV.h, VL_V2RSTD[i])
                if i >= 2:
                    v.wait_ge(sTB2[i % 2].h, TBX(i - 2))
                vch(v, v.tensor_scalar_add(z_sc[:, :], xt[:, :], st_nm[:, :]))
                v.tensor_scalar_mul(xln_b2[i % 2][:, :], z_sc[:, :],
                                    st_rstd[:, :]).then_inc(sV.h, 1)
                sV.n += 1
                assert sV.n == VL_V2XLN[i]

            def x2(i, g, tt):
                v.wait_ge(sM.h, m_ffn2(g, tt))
                if i >= 2:
                    v.wait_ge(sO2[i % 2].h, SO2(i - 2))
                v.tensor_add(outsb[i % 2][:, :], o2_ps[i % 2][:, :],
                             xf[i % 4][:, :]).then_inc(sV.h, 1)
                sV.n += 1
                assert sV.n == VL_V2X2[i]

            v.wait_ge(sM.h, M_B2BC)
            v.tensor_copy(arep2[:, :], o2_ps[0][:, :]).then_inc(sV.h, 1)
            sV.n += 1
            assert sV.n == V_AREP2
            v.wait_ge(sV.h, sV.n)
            for gg in range(2):
                for tt in range(GNT):
                    ln2(GNT * gg + tt)
            for k in range(NHC):
                v.wait_ge(sBG[k % 2].h, BG(32 + k))
                vch(v, v.tensor_scalar(z_sc[:, :], binf2[k % 2][:, :], 1e38,
                                       1.0, ALU.mult, ALU.min))
                v.tensor_scalar_max(w2b[k][:, :], z_sc[:, :], -1.0
                                    ).then_inc(sV.h, 1)
                sV.n += 1
                assert sV.n == VL_BIN2[k]
            for hc in range(NHC):
                v.tensor_tensor(w2b[hc][:, :], w2b[hc][:, :], arep2[:, :],
                                ALU.mult).then_inc(sV.h, 1)
                sV.n += 1
                assert sV.n == VL_M2[hc]
            for g in range(NG):
                for tt in range(GNT):
                    x2(GNT * g + tt, g, tt)
                if g + 2 < NG:
                    for tt in range(GNT):
                        ln2(GNT * (g + 2) + tt)
            assert sV.n == V_FINAL

        blockB.vector(v_B2)

        def sy_B2(sy):
            def load(i):
                sy.wait_ge(sO1[i % 2].h, SO1(i))
                if i >= 4:
                    sy.wait_ge(sV.h, VL_V2X2[i - 4])
                    sy.wait_ge(sA.h, SA2_SSQ[i - 4])
                else:
                    sy.wait_ge(sV.h, v_x1(LAST_A_XF[i]))
                sy.dma_start(out=xf[i % 4][:, :],
                             in_=x1_d.ap()[128 * i:128 * (i + 1), :]
                             ).then_inc(sXF[i % 4].h, 16)
                sXF[i % 4].n += 16

            for i in range(2 * GNT):
                load(i)
            for g in range(NG):
                for tt in range(GNT):
                    i = GNT * g + tt
                    sy.wait_ge(sV.h, VL_V2X2[i])
                    sy.dma_start(out=out_d.ap()[128 * i:128 * (i + 1), :],
                                 in_=outsb[i % 2][:, :]).then_inc(sO2[i % 2].h,
                                                                  16)
                    sO2[i % 2].n += 16
                if g + 2 < NG:
                    for tt in range(GNT):
                        load(GNT * (g + 2) + tt)
            sy.wait_ge(sO2[0].h, 128)
            sy.wait_ge(sO2[1].h, 128)

        blockB.sync(sy_B2)

        def t_B2(t):
            t.wait_ge(sV.h, v_x1(NT - 1))
            t.wait_ge(sG.h, G_A2R)
            last = None
            for n in range(2):
                last = t.matmul(o2_ps[0][:, 512 * n:512 * (n + 1)],
                                onesf[0:1, :],
                                a2row[0:1, 512 * n:512 * (n + 1)],
                                start=True, stop=True, skip_group_check=True)
            last.then_inc(sM.h, 1)
            sM.n += 1
            assert sM.n == M_B2BC
            for g in range(NG):
                t.wait_ge(sTB2[0].h, TBX(GNT * g))
                t.wait_ge(sTB2[1].h, TBX(GNT * g + 1))
                for hc in range(NHC):
                    if g == 0:
                        t.wait_ge(sV.h, VL_BIN1[8 * (hc // 8) + 7])
                    if hc >= 2:
                        t.wait_ge(sA.h, SA2_GELU[(g, hc - 2)])
                    elif g >= 1:
                        t.wait_ge(sA.h, SA2_GELU[(g - 1, NHC - 2 + hc)])
                    last = None
                    for dc in range(DC):
                        last = t.matmul(h_ps[hc % 2][:, :],
                                        bw1[dc][:, 128 * hc:128 * (hc + 1)],
                                        xl2T[g % 2][:, GT * dc:GT * (dc + 1)],
                                        start=(dc == 0), stop=(dc == DC - 1),
                                        skip_group_check=True)
                    last.then_inc(sM.h, 1)
                    sM.n += 1
                t.wait_ge(sA.h, SA2_GELU[(g, NHC - 1)])
                for tt in range(GNT):
                    i = GNT * g + tt
                    if i >= 2:
                        t.wait_ge(sV.h, VL_V2X2[i - 2])
                    for hc in range(NHC):
                        t.wait_ge(sV.h, VL_M2[hc])
                        for n in range(2):
                            t.matmul(o2_ps[i % 2][:, 512 * n:512 * (n + 1)],
                                     hTb[:, GT * hc + 128 * tt:
                                         GT * hc + 128 * (tt + 1)],
                                     w2b[hc][:, 512 * n:512 * (n + 1)],
                                     start=(hc == 0), stop=False,
                                     skip_group_check=True)
                    last = None
                    for n in range(2):
                        last = t.matmul(
                            o2_ps[i % 2][:, 512 * n:512 * (n + 1)],
                            onesr[0:1, :],
                            b2b[0:1, 512 * n:512 * (n + 1)],
                            start=False, stop=True, skip_group_check=True)
                    last.then_inc(sM.h, 1)
                    sM.n += 1

        blockB.tensor(t_B2)

    return nc


_NC_CACHE = {}


def _get_nc():
    if "nc" not in _NC_CACHE:
        _NC_CACHE["nc"] = build(debug=False)
    return _NC_CACHE["nc"]


def kernel(L0_attn_x, L0_attn_w_qkv, L0_attn_w_out, L0_ffn_w1, L0_ffn_b1,
           L0_ffn_w2, L0_ffn_b2):
    x = np.ascontiguousarray(np.asarray(L0_attn_x, dtype=np.float32))
    wq = np.asarray(L0_attn_w_qkv, dtype=np.float32)
    wo = np.asarray(L0_attn_w_out, dtype=np.float32)
    w1 = np.ascontiguousarray(np.asarray(L0_ffn_w1, dtype=np.float32))
    b1 = np.asarray(L0_ffn_b1, dtype=np.float32)
    w2 = np.ascontiguousarray(np.asarray(L0_ffn_w2, dtype=np.float32))
    b2 = np.asarray(L0_ffn_b2, dtype=np.float32)

    wqT = np.ascontiguousarray(wq.T)
    woT = np.ascontiguousarray(wo.T)
    w1T = np.ascontiguousarray(w1.T)
    w2T = np.ascontiguousarray(w2.T)
    b1r = np.ascontiguousarray(b1.reshape(1, FH))
    b2r = np.ascontiguousarray(b2.reshape(1, D))

    nc = _get_nc()
    in_maps = []
    for c in range(8):
        b, q = divmod(c, 4)
        in_maps.append({
            "x": np.ascontiguousarray(x[b, T * q:T * (q + 1), :]),
            "wqkvT": wqT, "woutT": woT, "w1T": w1T, "w2T": w2T,
            "w1n": w1, "w2n": w2, "b1": b1r, "b2": b2r,
            "tbase": np.full((128, 1), float(T * q), np.float32),
        })
    res = run_bass_kernel_spmd(nc, in_maps, core_ids=list(range(8)))
    out = np.empty((2, 8192, 1024), np.float32)
    for c in range(8):
        b, q = divmod(c, 4)
        out[b, T * q:T * (q + 1), :] = res.results[c]["out"]
    return out
